# version 12
# baseline (speedup 1.0000x reference)
import contextlib
import os
import numpy as np
import ml_dtypes
import concourse.bass as bass
import concourse.mybir as mybir
from concourse.bass_utils import run_bass_kernel_spmd
from concourse.replica_groups import maybe_share_collective_output_space

F32 = mybir.dt.float32
BF16 = mybir.dt.bfloat16
U32 = mybir.dt.uint32
AF = mybir.ActivationFunctionType
ALU = mybir.AluOpType
BFnp = ml_dtypes.bfloat16

NCORES = 8
RG = [list(range(NCORES))]

T = 16384
DM = 2048
DI = 4096
DS = DI // NCORES
NDG = DS // 128
S = 16
H = 5461
HP = 5504
NHG = HP // 128
TC = T // NCORES
CH = 2048
NCH = T // CH
SW = 2048
NSW = T // SW
NSUB = SW // 512
KDM = DM // 128
WCNT = float(H * DM)


_LND = {}


def build_nc(upto=int(os.environ.get("KUPTO", "99"))):
    nc = bass.Bass("TRN2", target_bir_lowering=False)

    x_all = nc.dram_tensor("x_all", [DM, T], F32, kind="ExternalInput")
    x_res = nc.dram_tensor("x_res", [DM, TC], F32, kind="ExternalInput")
    w_in = nc.dram_tensor("w_in", [DM, 2 * DS], F32, kind="ExternalInput")
    convw = nc.dram_tensor("convw", [128, 4 * NDG], F32, kind="ExternalInput")
    convb = nc.dram_tensor("convb", [128, NDG], F32, kind="ExternalInput")
    a_mat = nc.dram_tensor("a_mat", [128, NDG * S], F32, kind="ExternalInput")
    d_vec = nc.dram_tensor("d_vec", [128, NDG], F32, kind="ExternalInput")
    dt_bv = nc.dram_tensor("dt_bv", [128, NDG], F32, kind="ExternalInput")
    w_xp = nc.dram_tensor("w_xp", [DS, 160], F32, kind="ExternalInput")
    w_dt = nc.dram_tensor("w_dt", [128, DS], F32, kind="ExternalInput")
    w_outb = nc.dram_tensor("w_outb", [16, 32, 128, 128], F32, kind="ExternalInput")
    w1b = nc.dram_tensor("w1b", [NHG, KDM, 128, 128], F32, kind="ExternalInput")
    w3b = nc.dram_tensor("w3b", [NHG, KDM, 128, 128], F32, kind="ExternalInput")
    w2b = nc.dram_tensor("w2b", [NHG, 128, DM], F32, kind="ExternalInput")
    w1p = nc.dram_tensor("w1p", [128, 10922], F32, kind="ExternalInput")
    w3p = nc.dram_tensor("w3p", [128, 10922], F32, kind="ExternalInput")
    w2p = nc.dram_tensor("w2p", [128, 10922], F32, kind="ExternalInput")
    tok0 = nc.dram_tensor("tok0", [1, 1], U32, kind="ExternalInput")

    out_d = nc.dram_tensor("out_d", [DM, TC], F32, kind="ExternalOutput")

    xt_dr = nc.dram_tensor("xt_dr", [NDG, 128, T], BF16, kind="Internal")
    sg_dr = nc.dram_tensor("sg_dr", [NDG, 128, T], BF16, kind="Internal")
    ssm_in = nc.dram_tensor("ssm_in", [4, 160, T // 4], F32, kind="Internal")
    ssm_fl = nc.dram_tensor(
        "ssm_fl", [4, 160, T // 4], F32, kind="Internal",
        addr_space=maybe_share_collective_output_space("AllReduce", RG, prefer_local=True))
    abs_in = nc.dram_tensor("abs_in", [1, 8], F32, kind="Internal")
    abs_out = nc.dram_tensor(
        "abs_out", [1, 8], F32, kind="Internal",
        addr_space=maybe_share_collective_output_space("AllReduce", RG, prefer_local=True))
    t1_dr = nc.dram_tensor("t1_dr", [NHG, KDM, 128, 128], BF16, kind="Internal")
    t3_dr = nc.dram_tensor("t3_dr", [NHG, KDM, 128, 128], BF16, kind="Internal")
    t2_dr = nc.dram_tensor("t2_dr", [NHG, 128, DM], BF16, kind="Internal")
    y2_in = nc.dram_tensor("y2_in", [NCH, DS, CH], BF16, kind="Internal")
    y2_fl = nc.dram_tensor(
        "y2_fl", [NCH, DI, CH], BF16, kind="Internal",
        addr_space=maybe_share_collective_output_space("AllGather", RG, prefer_local=True))
    y2my = nc.dram_tensor("y2my", [DI, TC], BF16, kind="Internal")
    rs_dr = nc.dram_tensor("rs_dr", [DM, TC], F32, kind="Internal")
    xnf_dr = nc.dram_tensor("xnf_dr", [DM, TC], F32, kind="Internal")
    xnb_dr = nc.dram_tensor("xnb_dr", [DM, TC], BF16, kind="Internal")
    rs2_dr = nc.dram_tensor("rs2_dr", [DM, TC], F32, kind="Internal")
    dbg_g_dr = nc.dram_tensor("dbg_g_dr", [128, 512], BF16, kind="Internal")
    dbg_ps_dr = nc.dram_tensor("dbg_ps_dr", [128, 512], F32, kind="Internal")
    dbg_ps2_dr = nc.dram_tensor("dbg_ps2_dr", [128, 512], F32, kind="Internal")
    bc_dr = nc.dram_tensor("bc_dr", [32, T], BF16, kind="Internal")

    stack = contextlib.ExitStack()
    sb = lambda name, shape, dt: stack.enter_context(nc.sbuf_tensor(name, shape, dt))
    sem = lambda name: stack.enter_context(nc.semaphore(name))

    cw_sb = sb("cw_sb", [128, 4 * NDG], F32)
    cb_sb = sb("cb_sb", [128, NDG], F32)
    a_sb = sb("a_sb", [128, NDG * S], F32)
    dv_sb = sb("dv_sb", [128, NDG], F32)
    db_sb = sb("db_sb", [128, NDG], F32)
    xp_sb = sb("xp_sb", [128, NDG * 160], BF16)
    dtw_sb = sb("dtw_sb", [128, DS], BF16)
    onesb_sb = sb("onesb_sb", [128, 1], BF16)
    thp_sb = sb("thp_sb", [128, 3], F32)
    thn_sb = sb("thn_sb", [128, 3], F32)
    s1_sb = sb("s1_sb", [128, 1], F32)
    s23_sb = sb("s23_sb", [128, 1], F32)
    one_sb = sb("one_sb", [128, 1], F32)
    zero_sb = sb("zero_sb", [128, 1], F32)
    eps_sb = sb("eps_sb", [1, 1], F32)

    PH = sem("PH")
    ph_n = [0]

    ph_count = [0]

    def phase(tag, bodies):
        if ph_count[0] >= upto:
            ph_count[0] += 1
            return
        ph_count[0] += 1
        base = ph_n[0]
        with nc.Block(tag) as blk:
            for ename in ("sync", "gpsimd", "tensor", "scalar", "vector"):
                body = bodies.get(ename)

                def make(body=body):
                    def _(eng):
                        eng.wait_ge(PH, base)
                        if body is not None:
                            body(eng)
                        eng.nop().then_inc(PH, 1)
                    return _
                getattr(blk, ename)(make())
        ph_n[0] = base + 5

    ld0 = sem("ld0")

    def p0_sync(e):
        e.dma_start(out=cw_sb[:, :], in_=convw[:, :]).then_inc(ld0, 16)
        e.dma_start(out=cb_sb[:, :], in_=convb[:, :]).then_inc(ld0, 16)
        e.dma_start(out=a_sb[:, :], in_=a_mat[:, :]).then_inc(ld0, 16)
        e.dma_start(out=dv_sb[:, :], in_=d_vec[:, :]).then_inc(ld0, 16)
        e.dma_start(out=db_sb[:, :], in_=dt_bv[:, :]).then_inc(ld0, 16)
        e.wait_ge(ld0, 16 * 7)

    def p0_gp(e):
        e.dma_start(out=xp_sb.rearrange("p (g c) -> p g c", g=NDG),
                    in_=w_xp.rearrange("(g p) c -> p g c", p=128)).then_inc(ld0, 16)
        e.dma_start(out=dtw_sb[:, :], in_=w_dt[:, :]).then_inc(ld0, 16)
        e.wait_ge(ld0, 16 * 7)

    def p0_vec(e):
        e.memset(onesb_sb[:, :], 1.0)
        e.memset(one_sb[:, :], 1.0)
        e.memset(zero_sb[:, :], 0.0)
        e.memset(eps_sb[:, :], 1e-5)

    phase("p0", {"sync": p0_sync, "gpsimd": p0_gp, "vector": p0_vec})

    with contextlib.ExitStack() as _st:
        _en = _st.enter_context
        wp_sb = _en(nc.sbuf_tensor("wp_sb", [128, 10922], F32))
        wd_sb = _en(nc.sbuf_tensor("wd_sb", [128, 10922], F32))
        acol_sb = _en(nc.sbuf_tensor("acol_sb", [128, 3], F32))
        asum_sb = _en(nc.sbuf_tensor("asum_sb", [3, 1], F32))
        onesf_sb = _en(nc.sbuf_tensor("onesf_sb", [128, 1], F32))
        aps = _en(nc.psum_tensor("aps", [3, 1], F32))
        ldb = sem("ldb")
        scb = sem("scb")
        ccb = sem("ccb")
        rdb = sem("rdb")
        def pb_sync(e):
            for i, src in enumerate((w1p, w3p, w2p)):
                if i >= 1:
                    e.wait_ge(scb, i)
                e.dma_start(out=wp_sb[:, :], in_=src[:, :]).then_inc(ldb, 16)

        def pb_scalar(e):
            for i in range(3):
                e.wait_ge(ldb, 16 * (i + 1))
                if i >= 1:
                    e.wait_ge(rdb, i)
                e.activation(wd_sb[:, :], wp_sb[:, :], AF.Abs).then_inc(scb, 1)

        def pb_vec(e):
            e.memset(onesf_sb[:, :], 1.0)
            for i in range(3):
                e.wait_ge(scb, i + 1)
                e.tensor_reduce(acol_sb[:, i:i + 1], wd_sb[:, :],
                                mybir.AxisListType.X, ALU.add).then_inc(rdb, 1)
            e.wait_ge(rdb, 4)
            e.tensor_copy(asum_sb[:, :], aps[:, :]).then_inc(rdb, 1)
            e.wait_ge(ccb, 65)
            e.tensor_scalar(thp_sb[:, 0:3], thp_sb[:, 0:3], 0.5 / WCNT, None,
                            ALU.mult)
            for _ in range(8):
                e.nop()
            e.tensor_scalar(thn_sb[:, 0:3], thp_sb[:, 0:3], -1.0, None, ALU.mult)
            e.tensor_scalar(s1_sb[:, :], thp_sb[:, 0:1], -2.0, None, ALU.mult)
            e.tensor_tensor(s23_sb[:, :], thp_sb[:, 1:2], thp_sb[:, 2:3], ALU.mult)
            for _ in range(8):
                e.nop()
            e.tensor_scalar(s23_sb[:, :], s23_sb[:, :], 4.0, None, ALU.mult)

        def pb_tensor(e):
            e.wait_ge(rdb, 3)
            e.matmul(aps[:, :], acol_sb[:, :], onesf_sb[:, :],
                     start=True, stop=True).then_inc(rdb, 1)

        def pb_gp(e):
            e.wait_ge(rdb, 5)
            e.dma_start(out=bass.AP(abs_in, 0, [[1, 1], [1, 3]]),
                        in_=asum_sb[:, :]).then_inc(ccb, 16)
            e.wait_ge(ccb, 16)
            e.collective_compute(
                "AllReduce", ALU.add, replica_groups=RG,
                ins=[abs_in[:, :]], outs=[abs_out[:, :]]).then_inc(ccb, 1)
            e.wait_ge(ccb, 17)
            for j in range(3):
                e.dma_start(out=thp_sb[:, j:j + 1],
                            in_=bass.AP(abs_out, j, [[0, 128], [1, 1]])
                            ).then_inc(ccb, 16)
            e.wait_ge(ccb, 65)

        phase("p0b", {"sync": pb_sync, "scalar": pb_scalar, "tensor": pb_tensor,
                      "gpsimd": pb_gp, "vector": pb_vec})

    with contextlib.ExitStack() as _st:
        _en = _st.enter_context
        wi_sb = _en(nc.sbuf_tensor("wi_sb", [128, KDM * 1024], BF16))
        xb_sb = _en(nc.sbuf_tensor("xb_sb", [128, KDM * SW], BF16))
        cv_sb = _en(nc.sbuf_tensor("cv_sb", [128, NDG * (SW + 3)], F32))
        cvac_sb = _en(nc.sbuf_tensor("cvac_sb", [128, SW], F32))
        xtw_sb = _en(nc.sbuf_tensor("xtw_sb", [128, NDG * SW], BF16))
        sgw_sb = _en(nc.sbuf_tensor("sgw_sb", [128, NDG * SW], BF16))
        sse_sb = _en(nc.sbuf_tensor("sse_sb", [128, 512], F32))
        ssb_sb = _en(nc.sbuf_tensor("ssb_sb", [32, 512], F32))
        ip_ps0 = _en(nc.psum_tensor("ip_ps0", [128, 512], F32))
        ip_ps1 = _en(nc.psum_tensor("ip_ps1", [128, 512], F32))
        st_ps = _en(nc.psum_tensor("st_ps", [128, 512], F32))
        sb_ps = _en(nc.psum_tensor("sb_ps", [32, 512], F32))
        l1w = sem("l1w")
        l1xp = [sem("l1xa"), sem("l1xb")]
        m1 = sem("m1")
        e1 = sem("e1")
        v1 = sem("v1")
        x1 = sem("x1")
        sm1 = sem("sm1")
        se1 = sem("se1")
        st_xt = sem("st_xt")
        st_sg = sem("st_sg")
        stq = [sem(f"stq{i}") for i in range(4)]
        cc1 = sem("cc1")
        def p1_gp(e):
            e.dma_start(out=wi_sb.rearrange("p (k e) -> p k e", k=KDM),
                        in_=w_in.rearrange("(k p) e -> p k e", p=128)).then_inc(l1w, 16)
            for li in range(2 * NSW):
                if li >= 2:
                    e.wait_ge(m1, 16 * (li - 1))
                pr = li % 2
                e.dma_start(
                    out=xb_sb[:, pr * KDM * 1024:(pr + 1) * KDM * 1024]
                    .rearrange("p (k t) -> p k t", k=KDM),
                    in_=x_all.rearrange("(k p) t -> p k t", p=128)[
                        :, :, li * 1024:(li + 1) * 1024]).then_inc(l1xp[li % 2], 16)
            for q in range(4):
                e.wait_ge(stq[q], 16 * 16)
                e.collective_compute(
                    "AllReduce", ALU.add, replica_groups=RG,
                    ins=[ssm_in[q]], outs=[ssm_fl[q]]).then_inc(cc1, 1)
            e.wait_ge(cc1, 4)

        def p1_tensor(e):
            e.wait_ge(l1w, 16)
            for sw in range(NSW):
                for sub in range(NSUB):
                    li = sw * 2 + sub // 2
                    pr = li % 2
                    e.wait_ge(l1xp[pr], 16 * (li // 2 + 1))
                    for eg in range(8):
                        mg = sw * 32 + sub * 8 + eg
                        ps = ip_ps0 if mg % 2 == 0 else ip_ps1
                        if mg >= 2:
                            e.wait_ge(e1, mg - 1)
                        xoff = pr * KDM * 1024 + (sub % 2) * 512
                        for k in range(KDM):
                            e.matmul(
                                ps[:, :],
                                wi_sb[:, k * 1024 + eg * 128:k * 1024 + (eg + 1) * 128],
                                xb_sb[:, xoff + k * 1024:xoff + k * 1024 + 512],
                                start=(k == 0), stop=(k == KDM - 1),
                            ).then_inc(m1, 1 if k == KDM - 1 else 0)
                e.wait_ge(x1, (sw + 1) * NDG)
                for sub in range(NSUB):
                    gidx = sw * NSUB + sub
                    if gidx >= 1:
                        e.wait_ge(se1, gidx)
                    for dg in range(NDG):
                        xt_ap = xtw_sb[:, dg * SW + sub * 512:dg * SW + (sub + 1) * 512]
                        e.matmul(st_ps[:, :], xp_sb[:, dg * 160:dg * 160 + 128],
                                 xt_ap, start=(dg == 0), stop=(dg == NDG - 1))
                        e.matmul(sb_ps[:, :], xp_sb[:, dg * 160 + 128:(dg + 1) * 160],
                                 xt_ap, start=(dg == 0), stop=(dg == NDG - 1)
                                 ).then_inc(sm1, 1 if dg == NDG - 1 else 0)

        def p1_scalar(e):
            for sw in range(NSW):
                for sub in range(NSUB):
                    for eg in range(8):
                        mg = sw * 32 + sub * 8 + eg
                        ps = ip_ps0 if mg % 2 == 0 else ip_ps1
                        e.wait_ge(m1, mg + 1)
                        if eg < 4:
                            dg = eg
                            if sw >= 1 and sub == 0:
                                e.wait_ge(v1, sw * NDG)
                            e.copy(
                                cv_sb[:, dg * (SW + 3) + 3 + sub * 512:
                                      dg * (SW + 3) + 3 + (sub + 1) * 512],
                                ps[:, :]).then_inc(e1, 1)
                        else:
                            dg = eg - 4
                            if sw >= 1 and sub == 0:
                                e.wait_ge(st_sg, 16 * 4 * sw)
                            e.activation(
                                sgw_sb[:, dg * SW + sub * 512:dg * SW + (sub + 1) * 512],
                                ps[:, :], AF.Silu).then_inc(e1, 1)
                for dg in range(NDG):
                    e.wait_ge(v1, sw * NDG + dg + 1)
                    if sw >= 1:
                        e.wait_ge(sm1, sw * NSUB)
                        e.wait_ge(st_xt, 16 * 4 * sw)
                    e.activation(
                        xtw_sb[:, dg * SW:(dg + 1) * SW],
                        cv_sb[:, dg * (SW + 3) + 3:dg * (SW + 3) + 3 + SW],
                        AF.Silu, bias=cb_sb[:, dg:dg + 1]).then_inc(x1, 1)
                for sub in range(NSUB):
                    gidx = sw * NSUB + sub
                    e.wait_ge(sm1, gidx + 1)
                    if gidx >= 1:
                        pq = (gidx - 1) // 8
                        e.wait_ge(stq[pq], 16 * 2 * ((gidx - 1) % 8 + 1))
                    e.copy(sse_sb[:, :], st_ps[:, :])
                    e.copy(ssb_sb[:, :], sb_ps[:, :]).then_inc(se1, 1)

        def p1_vector(e):
            for sw in range(NSW):
                for dg in range(NDG):
                    base = dg * (SW + 3)
                    if sw == 0:
                        e.memset(cv_sb[:, base:base + 3], 0.0)
                    e.wait_ge(e1, sw * 32 + 3 * 8 + dg + 1)
                    e.tensor_scalar(cvac_sb[:, :], cv_sb[:, base:base + SW],
                                    cw_sb[:, dg * 4:dg * 4 + 1], None, ALU.mult)
                    for j in (1, 2, 3):
                        e.scalar_tensor_tensor(
                            cvac_sb[:, :], cv_sb[:, base + j:base + j + SW],
                            cw_sb[:, dg * 4 + j:dg * 4 + j + 1], cvac_sb[:, :],
                            ALU.mult, ALU.add)
                    e.tensor_copy(cv_sb[:, base:base + 3],
                                  cv_sb[:, base + SW:base + SW + 3])
                    e.tensor_copy(cv_sb[:, base + 3:base + 3 + SW],
                                  cvac_sb[:, :]).then_inc(v1, 1)

        def p1_sync(e):
            for sw in range(NSW):
                for dg in range(NDG):
                    e.wait_ge(x1, sw * NDG + dg + 1)
                    e.dma_start(out=xt_dr[dg, :, sw * SW:(sw + 1) * SW],
                                in_=xtw_sb[:, dg * SW:(dg + 1) * SW]).then_inc(st_xt, 16)
                for dg in range(NDG):
                    e.wait_ge(e1, sw * 32 + 32)
                    e.dma_start(out=sg_dr[dg, :, sw * SW:(sw + 1) * SW],
                                in_=sgw_sb[:, dg * SW:(dg + 1) * SW]).then_inc(st_sg, 16)
                for sub in range(NSUB):
                    gidx = sw * NSUB + sub
                    e.wait_ge(se1, gidx + 1)
                    t0 = sw * SW + sub * 512
                    q, qo = divmod(t0, T // 4)
                    e.dma_start(out=ssm_in[q, 0:128, qo:qo + 512],
                                in_=sse_sb[:, :]).then_inc(stq[q], 16)
                    e.dma_start(out=ssm_in[q, 128:160, qo:qo + 512],
                                in_=ssb_sb[:, :]).then_inc(stq[q], 16)
            e.wait_ge(st_xt, 16 * 4 * NSW)
            e.wait_ge(st_sg, 16 * 4 * NSW)
            for q in range(4):
                e.wait_ge(stq[q], 16 * 16)

        phase("p1", {"gpsimd": p1_gp, "tensor": p1_tensor, "scalar": p1_scalar,
                     "vector": p1_vector, "sync": p1_sync})

    qjobs_all = ([(w1b[hg].rearrange("k p c -> p k c"),
                   t1_dr[hg].rearrange("k p c -> p k c"), 0) for hg in range(NHG)]
                 + [(w3b[hg].rearrange("k p c -> p k c"),
                     t3_dr[hg].rearrange("k p c -> p k c"), 1) for hg in range(NHG)]
                 + [(w2b[hg][:, :], t2_dr[hg][:, :], 2) for hg in range(NHG)])

    with contextlib.ExitStack() as _st:
        _en = _st.enter_context
        dt_sb = _en(nc.sbuf_tensor("dt_sb", [128, NDG * CH], F32))
        xtc_sb = _en(nc.sbuf_tensor("xtc_sb", [128, NDG * CH], BF16))
        sgc_sb = _en(nc.sbuf_tensor("sgc_sb", [128, NDG * CH], BF16))
        wv_sb = _en(nc.sbuf_tensor("wv_sb", [128, NDG * CH], BF16))
        ts_sb = _en(nc.sbuf_tensor("ts_sb", [128, CH * 2], BF16))
        bbc_sb = _en(nc.sbuf_tensor("bbc_sb", [128, CH * 2], BF16))
        cbc_sb = _en(nc.sbuf_tensor("cbc_sb", [128, CH * 2], BF16))
        da_sb = _en(nc.sbuf_tensor("da_sb", [128, CH * 2], F32))
        dbx_sb = _en(nc.sbuf_tensor("dbx_sb", [128, CH], BF16))
        so_sb = _en(nc.sbuf_tensor("so_sb", [128, CH], BF16))
        tmp_sb = _en(nc.sbuf_tensor("tmp_sb", [128, CH * 2], BF16))
        ya_sb = _en(nc.sbuf_tensor("ya_sb", [128, NDG * CH], F32))
        stc_sb = _en(nc.sbuf_tensor("stc_sb", [128, NDG * S], F32))
        y2_sb = _en(nc.sbuf_tensor("y2_sb", [128, NDG * CH], BF16))
        spt_sb = _en(nc.sbuf_tensor("spt_sb", [128, 512], F32))
        dt_ps0 = _en(nc.psum_tensor("dt_ps0", [128, 512], F32))
        dt_ps1 = _en(nc.psum_tensor("dt_ps1", [128, 512], F32))
        l3g = sem("l3g")
        l3s = sem("l3s")
        m3 = sem("m3")
        sp3 = sem("sp3")
        vw3 = sem("vw3")
        bc3 = sem("bc3")
        ex3 = sem("ex3")
        vs3 = sem("vs3")
        acc = [sem("acc0"), sem("acc1")]
        vt3 = sem("vt3")
        g3 = sem("g3")
        vy3 = sem("vy3")
        st3c = [sem(f"st3c{i}") for i in range(NCH)]
        l3b = [sem(f"l3b{i}") for i in range(NCH)]
        cc3 = sem("cc3")
        NSUBC = CH // 512

        def p3_gp(e):
            for cb in range(NCH):
                qb, qob = divmod(cb * CH, T // 4)
                e.dma_start(out=bc_dr[:, cb * CH:(cb + 1) * CH],
                            in_=ssm_fl[qb, 128:160, qob:qob + CH]
                            ).then_inc(l3b[cb], 16)
            e.dma_start(out=ts_sb[:, 0:CH],
                        in_=ssm_fl[0, 0:128, 0:CH]).then_inc(l3g, 16)
            for c in range(NCH):
                for s in range(S):
                    for dg in range(NDG):
                        it = (c * S + s) * NDG + dg
                        pr = it % 2
                        if s == 8 and dg == 0 and c + 1 < NCH:
                            cn = c + 1
                            if cn >= 2:
                                e.wait_ge(m3, NDG * NSUBC * (cn - 1))
                            q, qo = divmod(cn * CH, T // 4)
                            e.dma_start(
                                out=ts_sb[:, (cn % 2) * CH:(cn % 2) * CH + CH],
                                in_=ssm_fl[q, 0:128, qo:qo + CH]
                                ).then_inc(l3g, 16)
                        e.wait_ge(vt3, it + 1)
                        if s == 0:
                            e.dma_start(
                                out=ya_sb[:, dg * CH:(dg + 1) * CH],
                                in_=tmp_sb[:, pr * CH:(pr + 1) * CH]
                                ).then_inc(acc[pr], 16)
                        else:
                            e.dma_start(
                                out=ya_sb[:, dg * CH:(dg + 1) * CH],
                                in_=tmp_sb[:, pr * CH:(pr + 1) * CH],
                                accum_op=ALU.add).then_inc(acc[pr], 16)
                e.wait_ge(st3c[c], 16 * NDG)
                e.collective_compute(
                    "AllGather", ALU.bypass, replica_groups=RG,
                    ins=[y2_in[c]], outs=[y2_fl[c]]).then_inc(cc3, 1)
            e.wait_ge(cc3, NCH)

        def p3_sync(e):
            for c in range(NCH):
                if c >= 1:
                    e.wait_ge(vy3, 2 * NDG * c)
                for dg in range(NDG):
                    e.dma_start(out=xtc_sb[:, dg * CH:(dg + 1) * CH],
                                in_=xt_dr[dg, :, c * CH:(c + 1) * CH]).then_inc(l3s, 16)
                    e.dma_start(out=sgc_sb[:, dg * CH:(dg + 1) * CH],
                                in_=sg_dr[dg, :, c * CH:(c + 1) * CH]).then_inc(l3s, 16)
                e.wait_ge(l3b[c], 16)
                for s in range(S):
                    gs = c * S + s
                    if gs >= 2:
                        e.wait_ge(vt3, (gs - 1) * NDG)
                    pr = s % 2
                    e.dma_start(
                        out=bbc_sb[:, pr * CH:(pr + 1) * CH],
                        in_=bass.AP(bc_dr, s * T + c * CH, [[0, 128], [1, CH]])
                        ).then_inc(bc3, 16)
                    e.dma_start(
                        out=cbc_sb[:, pr * CH:(pr + 1) * CH],
                        in_=bass.AP(bc_dr, (S + s) * T + c * CH, [[0, 128], [1, CH]])
                        ).then_inc(bc3, 16)
                for dg in range(NDG):
                    e.wait_ge(vy3, 2 * (c * NDG + dg + 1))
                    e.dma_start(
                        out=y2_in[c, dg * 128:(dg + 1) * 128, :],
                        in_=y2_sb[:, dg * CH:(dg + 1) * CH]).then_inc(st3c[c], 16)

        def p3_tensor(e):
            for c in range(NCH):
                e.wait_ge(l3g, 16 * (c + 1))
                for dg in range(NDG):
                    for sub in range(NSUBC):
                        gi = (c * NDG + dg) * NSUBC + sub
                        ps = dt_ps0 if gi % 2 == 0 else dt_ps1
                        if gi >= 2:
                            e.wait_ge(sp3, 2 * (gi - 1))
                        e.matmul(ps[:, :], dtw_sb[:, dg * 128:(dg + 1) * 128],
                                 ts_sb[:, (c % 2) * CH + sub * 512:
                                        (c % 2) * CH + (sub + 1) * 512],
                                 start=True, stop=True).then_inc(m3, 1)

        def p3_scalar(e):
            for c in range(NCH):
                for dg in range(NDG):
                    if c >= 1:
                        e.wait_ge(vw3, (c - 1) * NDG + dg + 1)
                    for sub in range(NSUBC):
                        gi = (c * NDG + dg) * NSUBC + sub
                        ps = dt_ps0 if gi % 2 == 0 else dt_ps1
                        e.wait_ge(m3, gi + 1)
                        e.activation(spt_sb[:, :], ps[:, :], AF.Exp,
                                     bias=db_sb[:, dg:dg + 1]).then_inc(sp3, 1)
                        e.activation(
                            dt_sb[:, dg * CH + sub * 512:dg * CH + (sub + 1) * 512],
                            spt_sb[:, :], AF.Ln, bias=one_sb[:, :]).then_inc(sp3, 1)
                for s in range(S):
                    for dg in range(NDG):
                        it = (c * S + s) * NDG + dg
                        pr = it % 2
                        if it >= 2:
                            e.wait_ge(vs3, it - 1)
                        e.activation(
                            da_sb[:, pr * CH:(pr + 1) * CH],
                            dt_sb[:, dg * CH:(dg + 1) * CH], AF.Exp,
                            scale=a_sb[:, dg * S + s:dg * S + s + 1]).then_inc(ex3, 1)

        def p3_vector(e):
            e.memset(stc_sb[:, :], 0.0)
            for c in range(NCH):
                e.wait_ge(l3s, 16 * 8 * (c + 1))
                for dg in range(NDG):
                    e.wait_ge(sp3, ((c * NDG + dg) * NSUBC + NSUBC) * 2)
                    e.tensor_tensor(wv_sb[:, dg * CH:(dg + 1) * CH],
                                    dt_sb[:, dg * CH:(dg + 1) * CH],
                                    xtc_sb[:, dg * CH:(dg + 1) * CH],
                                    ALU.mult).then_inc(vw3, 1)
                for s in range(S):
                    e.wait_ge(bc3, 32 * (c * S + s + 1))
                    prs = s % 2
                    for dg in range(NDG):
                        it = (c * S + s) * NDG + dg
                        pr = it % 2
                        e.tensor_tensor(dbx_sb[:, :], wv_sb[:, dg * CH:(dg + 1) * CH],
                                        bbc_sb[:, prs * CH:(prs + 1) * CH], ALU.mult)
                        e.wait_ge(ex3, it + 1)
                        col = dg * S + s
                        e.tensor_tensor_scan(
                            so_sb[:, :], da_sb[:, pr * CH:(pr + 1) * CH],
                            dbx_sb[:, :], stc_sb[:, col:col + 1],
                            ALU.mult, ALU.add).then_inc(vs3, 1)
                        if it >= 2:
                            e.wait_ge(acc[pr], 16 * (it // 2))
                        e.tensor_tensor(
                            tmp_sb[:, pr * CH:(pr + 1) * CH], so_sb[:, :],
                            cbc_sb[:, prs * CH:(prs + 1) * CH], ALU.mult).then_inc(vt3, 1)
                        e.tensor_copy(stc_sb[:, col:col + 1], so_sb[:, CH - 1:CH])
                if c >= 1:
                    e.wait_ge(st3c[c - 1], 16 * NDG)
                e.wait_ge(acc[0], 16 * 32 * (c + 1))
                e.wait_ge(acc[1], 16 * 32 * (c + 1))
                for dg in range(NDG):
                    e.scalar_tensor_tensor(
                        ya_sb[:, dg * CH:(dg + 1) * CH],
                        xtc_sb[:, dg * CH:(dg + 1) * CH],
                        dv_sb[:, dg:dg + 1],
                        ya_sb[:, dg * CH:(dg + 1) * CH],
                        ALU.mult, ALU.add).then_inc(vy3, 1)
                    e.tensor_tensor(
                        y2_sb[:, dg * CH:(dg + 1) * CH],
                        ya_sb[:, dg * CH:(dg + 1) * CH],
                        sgc_sb[:, dg * CH:(dg + 1) * CH], ALU.mult).then_inc(vy3, 1)

        phase("p3", {"gpsimd": p3_gp, "sync": p3_sync, "tensor": p3_tensor,
                     "scalar": p3_scalar, "vector": p3_vector})

    with contextlib.ExitStack() as _st:
        _en = _st.enter_context
        tk_sb = _en(nc.sbuf_tensor("tk_sb", [1, 1], U32))
        l4 = sem("l4")
        c4 = sem("c4")
        def p4_gp(e):
            e.dma_start(out=tk_sb[:, :], in_=tok0[:, :]).then_inc(l4, 16)
            e.wait_ge(l4, 16)
            with e.register("p4off") as roff, e.register("p4r0") as r0, \
                    e.register("p4r") as r4:
                e.load(roff, tk_sb[0:1, 0:1])
                e.reg_mul(r0, roff, DI)
                for j in range(8):
                    e.reg_add(r4, r0, j * 512 * CH)
                    e.dma_start(
                        out=y2my[j * 512:(j + 1) * 512, :],
                        in_=bass.AP(y2_fl, r4, [[CH, 512], [1, CH]]),
                    ).then_inc(c4, 16)
            e.wait_ge(c4, 16 * 8)

        phase("p4", {"gpsimd": p4_gp})

    with contextlib.ExitStack() as _st:
        _en = _st.enter_context
        yf_sb = _en(nc.sbuf_tensor("yf_sb", [128, 32 * TC], BF16))
        wo_sb = _en(nc.sbuf_tensor("wo_sb", [128, 2 * 32 * 128], BF16))
        xr_sb = _en(nc.sbuf_tensor("xr_sb", [128, 2 * 512], F32))
        rs_sb = _en(nc.sbuf_tensor("rs_sb", [128, 2 * 512], F32))
        op_ps0 = _en(nc.psum_tensor("op_ps0", [128, 512], F32))
        op_ps1 = _en(nc.psum_tensor("op_ps1", [128, 512], F32))
        l5y = sem("l5y")
        l5w = sem("l5w")
        l5x = sem("l5x")
        m5 = sem("m5")
        v5 = sem("v5")
        st5 = sem("st5")
        def p5_sync(e):
            for k in range(32):
                e.dma_start(out=yf_sb[:, k * TC:(k + 1) * TC],
                            in_=y2my[k * 128:(k + 1) * 128, :]).then_inc(l5y, 16)
            for gi in range(64):
                eg, ts_i = divmod(gi, 4)
                pr = gi % 2
                if gi >= 2:
                    e.wait_ge(v5, gi - 1)
                e.dma_start(
                    out=xr_sb[:, pr * 512:(pr + 1) * 512],
                    in_=x_res[eg * 128:(eg + 1) * 128, ts_i * 512:(ts_i + 1) * 512],
                ).then_inc(l5x, 16)
        def p5_st(e):
            for gi in range(64):
                eg, ts_i = divmod(gi, 4)
                pr = gi % 2
                e.wait_ge(v5, gi + 1)
                e.dma_start(
                    out=rs_dr[eg * 128:(eg + 1) * 128, ts_i * 512:(ts_i + 1) * 512],
                    in_=rs_sb[:, pr * 512:(pr + 1) * 512]).then_inc(st5, 16)
            e.wait_ge(st5, 16 * 64)

        def p5_gp(e):
            for eg in range(16):
                pr = eg % 2
                if eg >= 2:
                    e.wait_ge(m5, (eg - 1) * 4)
                e.dma_start(
                    out=wo_sb[:, pr * 4096:(pr + 1) * 4096].rearrange(
                        "p (k c) -> p k c", k=32),
                    in_=w_outb[eg].rearrange("k p c -> p k c")).then_inc(l5w, 16)

        def p5_tensor(e):
            e.wait_ge(l5y, 16 * 32)
            for eg in range(16):
                pr = eg % 2
                e.wait_ge(l5w, 16 * (eg + 1))
                for ts_i in range(4):
                    gi = eg * 4 + ts_i
                    ps = op_ps0 if gi % 2 == 0 else op_ps1
                    if gi >= 2:
                        e.wait_ge(v5, gi - 1)
                    for k in range(32):
                        e.matmul(
                            ps[:, :],
                            wo_sb[:, (pr * 32 + k) * 128:(pr * 32 + k + 1) * 128],
                            yf_sb[:, k * TC + ts_i * 512:k * TC + (ts_i + 1) * 512],
                            start=(k == 0), stop=(k == 31),
                        ).then_inc(m5, 1 if k == 31 else 0)

        def p5_vec(e):
            for gi in range(64):
                ps = op_ps0 if gi % 2 == 0 else op_ps1
                pr = gi % 2
                e.wait_ge(m5, gi + 1)
                e.wait_ge(l5x, 16 * (gi + 1))
                if gi >= 2:
                    e.wait_ge(st5, 16 * (gi - 1))
                e.tensor_tensor(rs_sb[:, pr * 512:(pr + 1) * 512], ps[:, :],
                                xr_sb[:, pr * 512:(pr + 1) * 512],
                                ALU.add).then_inc(v5, 1)

        phase("p5a", {"sync": p5_sync, "gpsimd": p5_gp, "tensor": p5_tensor,
                      "vector": p5_vec, "scalar": p5_st})

    def emit_ln(tag, src_dr, dst_f_dr, dst_b_dr, dbg=False, qjobs=None):
        n_ts, tw = 4, 512
        if qjobs:
            NJ = len(qjobs)
            qsched = [[] for _ in range(n_ts)]
            for j in range(NJ):
                qsched[min(j * n_ts // NJ, n_ts - 1)].append(j)
        rows_dr = nc.dram_tensor(tag + "rows", [2, tw], F32, kind="Internal")
        dbg = dbg and os.environ.get("KDEBUG") == "1"
        if dbg:
            lnd = {nm: nc.dram_tensor(tag + "lnd" + nm, [128, tw], F32,
                                      kind="Internal")
                   for nm in ("rt", "mb", "rs", "xo")}
            _LND.update(lnd)
            oD = sem(tag + "oD")
        VTOT = 18
        STOT = 33 + (16 if dst_b_dr is not None else 0)
        NOUT = 32 if dst_b_dr is not None else 16
        with contextlib.ExitStack() as _st:
            _en = _st.enter_context
            rt_sb = _en(nc.sbuf_tensor(tag + "rt", [128, 2 * KDM * tw], F32))
            rb_sb = _en(nc.sbuf_tensor(tag + "rb", [128, KDM * tw], BF16))
            sq_sb = _en(nc.sbuf_tensor(tag + "sq", [128, KDM * tw], BF16))
            mr_sb = _en(nc.sbuf_tensor(tag + "mr", [1, tw], F32))
            qr_sb = _en(nc.sbuf_tensor(tag + "qr", [1, tw], F32))
            m2_sb = _en(nc.sbuf_tensor(tag + "m2", [1, tw], F32))
            mb_sb = _en(nc.sbuf_tensor(tag + "mb", [128, tw], F32))
            rstb_sb = _en(nc.sbuf_tensor(tag + "rs", [128, tw], F32))
            xo_sb = _en(nc.sbuf_tensor(tag + "xo", [128, KDM * tw], F32))
            xb2_sb = _en(nc.sbuf_tensor(tag + "xb", [128, KDM * tw], BF16))
            sx_ps = _en(nc.psum_tensor(tag + "px", [1, tw], F32))
            sq_ps = _en(nc.psum_tensor(tag + "pq", [1, tw], F32))
            if qjobs:
                qw_sb = _en(nc.sbuf_tensor(tag + "qw", [128, 2048 * 2], F32))
                qp_sb = _en(nc.sbuf_tensor(tag + "qp", [128, 2048], BF16))
                qt_sb = _en(nc.sbuf_tensor(tag + "qt", [128, 2048 * 2], BF16))
                l2p = [sem(tag + "l2a"), sem(tag + "l2b")]
                v2 = sem(tag + "v2")
                s2p = [sem(tag + "s2a"), sem(tag + "s2b")]
            lA = sem(tag + "l")
            sA = sem(tag + "s")
            mA = sem(tag + "m")
            vA = sem(tag + "v")
            bA = sem(tag + "b")
            oA = sem(tag + "o")
            def b_sync(e):
                for ts_i in range(n_ts):
                    pr = ts_i % 2
                    if ts_i >= 2:
                        e.wait_ge(vA, (ts_i - 1) * VTOT)
                        e.wait_ge(sA, (ts_i - 1) * STOT + 32)
                    if dbg and ts_i == 1:
                        e.wait_ge(oD, 64)
                    for k in range(KDM):
                        e.dma_start(
                            out=rt_sb[:, (pr * KDM + k) * tw:(pr * KDM + k + 1) * tw],
                            in_=src_dr[k * 128:(k + 1) * 128,
                                       ts_i * tw:(ts_i + 1) * tw]).then_inc(lA, 16)
                    e.wait_ge(vA, ts_i * VTOT + 2)
                    e.dma_start(out=rows_dr[0:1, :], in_=mr_sb[:, :]).then_inc(bA, 16)
                    e.dma_start(out=rows_dr[1:2, :], in_=qr_sb[:, :]).then_inc(bA, 16)
                    e.wait_ge(bA, 64 * ts_i + 32)
                    e.dma_start(out=mb_sb[:, :],
                                in_=bass.AP(rows_dr, 0, [[0, 128], [1, tw]])
                                ).then_inc(bA, 16)
                    e.dma_start(out=rstb_sb[:, :],
                                in_=bass.AP(rows_dr, tw, [[0, 128], [1, tw]])
                                ).then_inc(bA, 16)
                    for k in range(KDM):
                        e.wait_ge(vA, ts_i * VTOT + 2 + k + 1)
                        e.dma_start(
                            out=dst_f_dr[k * 128:(k + 1) * 128,
                                         ts_i * tw:(ts_i + 1) * tw],
                            in_=xo_sb[:, k * tw:(k + 1) * tw]).then_inc(oA, 16)
                        if dst_b_dr is not None:
                            e.wait_ge(sA, ts_i * STOT + 33 + k + 1)
                            e.dma_start(
                                out=dst_b_dr[k * 128:(k + 1) * 128,
                                             ts_i * tw:(ts_i + 1) * tw],
                                in_=xb2_sb[:, k * tw:(k + 1) * tw]).then_inc(oA, 16)
                    if dbg and ts_i == 0:
                        e.wait_ge(vA, VTOT)
                        e.dma_start(out=lnd["rt"][:, :],
                                    in_=rt_sb[:, 0:tw]).then_inc(oD, 16)
                        e.dma_start(out=lnd["mb"][:, :],
                                    in_=mb_sb[:, :]).then_inc(oD, 16)
                        e.dma_start(out=lnd["rs"][:, :],
                                    in_=rstb_sb[:, :]).then_inc(oD, 16)
                        e.dma_start(out=lnd["xo"][:, :],
                                    in_=xo_sb[:, 0:tw]).then_inc(oD, 16)
                e.wait_ge(oA, 16 * NOUT * n_ts)
                if dbg:
                    e.wait_ge(oD, 64)

            def b_scalar(e):
                for ts_i in range(n_ts):
                    pr = ts_i % 2
                    for k in range(KDM):
                        e.wait_ge(lA, 16 * KDM * (ts_i + 1))
                        if ts_i >= 1:
                            e.wait_ge(mA, ts_i)
                        e.copy(rb_sb[:, k * tw:(k + 1) * tw],
                               rt_sb[:, (pr * KDM + k) * tw:(pr * KDM + k + 1) * tw]
                               ).then_inc(sA, 1)
                        e.activation(sq_sb[:, k * tw:(k + 1) * tw],
                                     rt_sb[:, (pr * KDM + k) * tw:(pr * KDM + k + 1) * tw],
                                     AF.Square).then_inc(sA, 1)
                    e.wait_ge(vA, ts_i * VTOT + 1)
                    e.activation(qr_sb[:, :], qr_sb[:, :], AF.Sqrt,
                                 bias=eps_sb[:, :]).then_inc(sA, 1)
                    if dst_b_dr is not None:
                        for k in range(KDM):
                            e.wait_ge(vA, ts_i * VTOT + 2 + k + 1)
                            e.copy(xb2_sb[:, k * tw:(k + 1) * tw],
                                   xo_sb[:, k * tw:(k + 1) * tw]).then_inc(sA, 1)

            def b_tensor(e):
                for ts_i in range(n_ts):
                    e.wait_ge(sA, ts_i * STOT + 32)
                    if ts_i >= 1:
                        e.wait_ge(vA, (ts_i - 1) * VTOT + 1)
                    for k in range(KDM):
                        e.matmul(sx_ps[:, :], onesb_sb[:, :],
                                 rb_sb[:, k * tw:(k + 1) * tw],
                                 start=(k == 0), stop=(k == KDM - 1))
                        e.matmul(sq_ps[:, :], onesb_sb[:, :],
                                 sq_sb[:, k * tw:(k + 1) * tw],
                                 start=(k == 0), stop=(k == KDM - 1)
                                 ).then_inc(mA, 1 if k == KDM - 1 else 0)

            def b_vec(e):
                for ts_i in range(n_ts):
                    e.wait_ge(mA, ts_i + 1)
                    if ts_i >= 1:
                        e.wait_ge(bA, 64 * (ts_i - 1) + 32)
                    e.tensor_scalar(mr_sb[:, :], sx_ps[:, :], 1.0 / DM, None,
                                    ALU.mult)
                    e.tensor_scalar(qr_sb[:, :], sq_ps[:, :], 1.0 / DM, None,
                                    ALU.mult)
                    for _ in range(4):
                        e.nop()
                    e.tensor_tensor(m2_sb[:, :], mr_sb[:, :], mr_sb[:, :], ALU.mult)
                    for _ in range(4):
                        e.nop()
                    e.tensor_tensor(qr_sb[:, :], qr_sb[:, :], m2_sb[:, :],
                                    ALU.subtract).then_inc(vA, 1)
                    e.wait_ge(sA, ts_i * STOT + 33)
                    e.reciprocal(qr_sb[:, :], qr_sb[:, :]).then_inc(vA, 1)
                    e.wait_ge(bA, 64 * (ts_i + 1))
                    if ts_i >= 1:
                        e.wait_ge(oA, 16 * NOUT * ts_i)
                    for k in range(KDM):
                        e.tensor_tensor(
                            xo_sb[:, k * tw:(k + 1) * tw],
                            rt_sb[:, ((ts_i % 2) * KDM + k) * tw:
                                  ((ts_i % 2) * KDM + k + 1) * tw],
                            mb_sb[:, :], ALU.subtract)
                    for k in range(KDM):
                        e.tensor_tensor(xo_sb[:, k * tw:(k + 1) * tw],
                                        xo_sb[:, k * tw:(k + 1) * tw],
                                        rstb_sb[:, :], ALU.mult).then_inc(vA, 1)
                    if qjobs:
                        for j in qsched[ts_i]:
                            half = (j % 2) * 2048
                            e.wait_ge(l2p[j % 2], 16 * (j // 2 + 1))
                            if j >= 2:
                                e.wait_ge(s2p[j % 2], 16 * (j // 2))
                            e.tensor_scalar(qp_sb[:, :], qw_sb[:, half:half + 2048],
                                            thp_sb[:, qjobs[j][2]:qjobs[j][2] + 1],
                                            None, ALU.is_gt)
                            e.scalar_tensor_tensor(
                                qt_sb[:, half:half + 2048],
                                qw_sb[:, half:half + 2048],
                                thn_sb[:, qjobs[j][2]:qjobs[j][2] + 1], qp_sb[:, :],
                                ALU.is_lt, ALU.subtract).then_inc(v2, 1)

            def b_gp(e):
                if not qjobs:
                    return
                NJq = len(qjobs)

                def q_store(j):
                    half = (j % 2) * 2048
                    e.wait_ge(v2, j + 1)
                    qsrc, qdst, mi = qjobs[j]
                    src_ap = qt_sb[:, half:half + 2048]
                    if len(qdst.shape) == 3:
                        src_ap = src_ap.rearrange("p (k c) -> p k c", k=KDM)
                    e.dma_start(out=qdst, in_=src_ap).then_inc(s2p[j % 2], 16)

                for j in range(NJq):
                    half = (j % 2) * 2048
                    if j >= 2:
                        e.wait_ge(v2, j - 1)
                    qsrc, qdst, mi = qjobs[j]
                    dst_ap = qw_sb[:, half:half + 2048]
                    if len(qsrc.shape) == 3:
                        dst_ap = dst_ap.rearrange("p (k c) -> p k c", k=KDM)
                    e.dma_start(out=dst_ap, in_=qsrc).then_inc(l2p[j % 2], 16)
                    if j >= 2:
                        q_store(j - 2)
                q_store(NJq - 2)
                q_store(NJq - 1)
                e.wait_ge(s2p[0], 16 * ((NJq + 1) // 2))
                e.wait_ge(s2p[1], 16 * (NJq // 2))

            phase(tag, {"sync": b_sync, "scalar": b_scalar, "tensor": b_tensor,
                        "vector": b_vec, "gpsimd": b_gp})

    emit_ln("p5b", rs_dr, xnf_dr, xnb_dr, qjobs=qjobs_all)

    for hf in range(2):
        HW = 1024
        with contextlib.ExitStack() as _st:
            _en = _st.enter_context
            xnh_sb = _en(nc.sbuf_tensor(f"f{hf}xn", [128, KDM * HW], BF16))
            g_sb = _en(nc.sbuf_tensor(f"f{hf}g", [128, NHG * HW], BF16))
            t1s_sb = _en(nc.sbuf_tensor(f"f{hf}t1", [128, 2 * 2048], BF16))
            t3s_sb = _en(nc.sbuf_tensor(f"f{hf}t3", [128, 2 * 2048], BF16))
            sil_sb = _en(nc.sbuf_tensor(f"f{hf}sil", [128, 4 * 512], BF16))
            t2s_sb = _en(nc.sbuf_tensor(f"f{hf}t2", [128, 2 * 512], BF16))
            xf_sb = _en(nc.sbuf_tensor(f"f{hf}xf", [128, 8 * 512], F32))
            r2_sb = _en(nc.sbuf_tensor(f"f{hf}r2", [128, 8 * 512], F32))
            fp0 = _en(nc.psum_tensor(f"f{hf}p0", [128, 512], F32))
            fp1 = _en(nc.psum_tensor(f"f{hf}p1", [128, 512], F32))
            fp2 = _en(nc.psum_tensor(f"f{hf}p2", [128, 512], F32))
            fp3 = _en(nc.psum_tensor(f"f{hf}p3", [128, 512], F32))
            fp4 = _en(nc.psum_tensor(f"f{hf}p4", [128, 512], F32))
            fp5 = _en(nc.psum_tensor(f"f{hf}p5", [128, 512], F32))
            fp6 = _en(nc.psum_tensor(f"f{hf}p6", [128, 512], F32))
            fp7 = _en(nc.psum_tensor(f"f{hf}p7", [128, 512], F32))
            dbg_sb = _en(nc.sbuf_tensor(f"f{hf}dbg", [128, 512], F32))
            d6 = sem(f"f{hf}d6")
            lx6 = sem(f"f{hf}lx")
            lt6 = sem(f"f{hf}lt")
            m6 = sem(f"f{hf}m")
            s6 = sem(f"f{hf}s")
            v6 = sem(f"f{hf}v")
            l26 = sem(f"f{hf}l2")
            lf6 = sem(f"f{hf}lf")
            m26 = sem(f"f{hf}m2")
            v26 = sem(f"f{hf}v2")
            st6 = sem(f"f{hf}st")
            hps = [fp0, fp1, fp2, fp3]
            hps2 = [fp4, fp5, fp6, fp7]
            wps = [fp0, fp1, fp2, fp3, fp4, fp5, fp6, fp7]

            def p6_sync(e):
                for k in range(KDM):
                    e.dma_start(
                        out=xnh_sb[:, k * HW:(k + 1) * HW],
                        in_=xnb_dr[k * 128:(k + 1) * 128,
                                   hf * HW:(hf + 1) * HW]).then_inc(lx6, 16)
                for hg in range(NHG):
                    pr = hg % 2
                    if hg >= 2:
                        e.wait_ge(m6, 4 * (hg - 1))
                    e.dma_start(out=t1s_sb[:, pr * 2048:(pr + 1) * 2048]
                                .rearrange("p (k c) -> p k c", k=KDM),
                                in_=t1_dr[hg].rearrange("k p c -> p k c")
                                ).then_inc(lt6, 16)
                    e.dma_start(out=t3s_sb[:, pr * 2048:(pr + 1) * 2048]
                                .rearrange("p (k c) -> p k c", k=KDM),
                                in_=t3_dr[hg].rearrange("k p c -> p k c")
                                ).then_inc(lt6, 16)
                for dp in range(4):
                    for hg in range(NHG):
                        gi = dp * NHG + hg
                        pr = gi % 2
                        if gi >= 2:
                            e.wait_ge(m26, gi - 1)
                        e.dma_start(
                            out=t2s_sb[:, pr * 512:(pr + 1) * 512],
                            in_=t2_dr[hg][:, dp * 512:(dp + 1) * 512]).then_inc(l26, 16)
                    for dgo in range(4):
                        for sub in range(2):
                            dgg = dp * 4 + dgo
                            si = dgo * 2 + sub
                            if dp >= 1:
                                e.wait_ge(v26, 8 * (dp - 1) + si + 1)
                            e.dma_start(
                                out=xf_sb[:, si * 512:(si + 1) * 512],
                                in_=xnf_dr[dgg * 128:(dgg + 1) * 128,
                                           hf * HW + sub * 512:hf * HW + (sub + 1) * 512],
                            ).then_inc(lf6, 16)

            def p6_tensor(e):
                e.wait_ge(lx6, 16 * KDM)
                for hg in range(NHG):
                    pr = hg % 2
                    e.wait_ge(lt6, 32 * (hg + 1))
                    pset = hps if pr == 0 else hps2
                    if hg >= 2:
                        e.wait_ge(v6, 2 * (hg - 1))
                    for sub in range(2):
                        for k in range(KDM):
                            e.matmul(
                                pset[sub][:, :],
                                t1s_sb[:, pr * 2048 + k * 128:pr * 2048 + (k + 1) * 128],
                                xnh_sb[:, k * HW + sub * 512:k * HW + (sub + 1) * 512],
                                start=(k == 0), stop=(k == KDM - 1)
                                ).then_inc(m6, 1 if k == KDM - 1 else 0)
                        for k in range(KDM):
                            e.matmul(
                                pset[2 + sub][:, :],
                                t3s_sb[:, pr * 2048 + k * 128:pr * 2048 + (k + 1) * 128],
                                xnh_sb[:, k * HW + sub * 512:k * HW + (sub + 1) * 512],
                                start=(k == 0), stop=(k == KDM - 1)
                                ).then_inc(m6, 1 if k == KDM - 1 else 0)
                e.wait_ge(v6, 2 * NHG)
                for dp in range(4):
                    if dp >= 1:
                        e.wait_ge(v26, 8 * dp)
                    for hg in range(NHG):
                        gi = dp * NHG + hg
                        pr = gi % 2
                        e.wait_ge(l26, 16 * (gi + 1))
                        for dgo in range(4):
                            for sub in range(2):
                                e.matmul(
                                    wps[dgo * 2 + sub][:, :],
                                    t2s_sb[:, pr * 512 + dgo * 128:
                                           pr * 512 + (dgo + 1) * 128],
                                    g_sb[:, hg * HW + sub * 512:hg * HW + (sub + 1) * 512],
                                    start=(hg == 0), stop=(hg == NHG - 1),
                                ).then_inc(m26, 1 if (dgo == 3 and sub == 1) else 0)

            def p6_scalar(e):
                for hg in range(NHG):
                    pr = hg % 2
                    pset = hps if pr == 0 else hps2
                    for sub in range(2):
                        e.wait_ge(m6, 4 * hg + 2 * (sub + 1))
                        if hg >= 2:
                            e.wait_ge(v6, 2 * (hg - 1))
                        e.activation(sil_sb[:, (pr * 2 + sub) * 512:
                                            (pr * 2 + sub + 1) * 512],
                                     pset[sub][:, :], AF.Silu,
                                     scale=s1_sb[:, :]).then_inc(s6, 1)

            def p6_vec(e):
                for hg in range(NHG):
                    pr = hg % 2
                    pset = hps if pr == 0 else hps2
                    for sub in range(2):
                        e.wait_ge(s6, 2 * hg + sub + 1)
                        e.wait_ge(m6, 4 * hg + 4)
                        e.tensor_tensor(
                            g_sb[:, hg * HW + sub * 512:hg * HW + (sub + 1) * 512],
                            sil_sb[:, (pr * 2 + sub) * 512:(pr * 2 + sub + 1) * 512],
                            pset[2 + sub][:, :], ALU.mult).then_inc(v6, 1)
                dbg_w2 = os.environ.get("KDEBUG") == "1" and hf == 0
                for dp in range(4):
                    if dp == 0 and dbg_w2:
                        e.wait_ge(m26, NHG)
                        e.tensor_copy(dbg_sb[:, :], wps[0][:, :]).then_inc(d6, 1)
                    if dp == 1 and dbg_w2:
                        e.wait_ge(m26, 2 * NHG)
                        e.wait_ge(d6, 33)
                        e.tensor_copy(dbg_sb[:, :], wps[2][:, :]).then_inc(d6, 1)
                    for dgo in range(4):
                        for sub in range(2):
                            si = dgo * 2 + sub
                            e.wait_ge(m26, (dp + 1) * NHG)
                            if si == 0:
                                e.wait_ge(lf6, 16 * 8 * (dp + 1))
                                if dp >= 1:
                                    e.wait_ge(st6, 16 * 8 * dp)
                            e.scalar_tensor_tensor(
                                r2_sb[:, si * 512:(si + 1) * 512],
                                wps[si][:, :], s23_sb[:, :],
                                xf_sb[:, si * 512:(si + 1) * 512],
                                ALU.mult, ALU.add).then_inc(v26, 1)

            def p6_sync2(e):
                for dp in range(4):
                    for dgo in range(4):
                        for sub in range(2):
                            si = dgo * 2 + sub
                            dgg = dp * 4 + dgo
                            e.wait_ge(v26, 8 * dp + si + 1)
                            e.dma_start(
                                out=rs2_dr[dgg * 128:(dgg + 1) * 128,
                                           hf * HW + sub * 512:hf * HW + (sub + 1) * 512],
                                in_=r2_sb[:, si * 512:(si + 1) * 512]).then_inc(st6, 16)
                e.wait_ge(st6, 16 * 32)

            def p6_scalar_all(e):
                p6_scalar(e)
                if os.environ.get("KDEBUG") == "1" and hf == 0:
                    e.wait_ge(v6, 2)
                    e.dma_start(out=dbg_g_dr[:, :], in_=g_sb[:, 0:512]).then_inc(d6, 16)
                    e.wait_ge(d6, 17)
                    e.dma_start(out=dbg_ps_dr[:, :], in_=dbg_sb[:, :]).then_inc(d6, 16)
                    e.nop().then_inc(d6, 1)
                    e.wait_ge(d6, 35)
                    e.dma_start(out=dbg_ps2_dr[:, :], in_=dbg_sb[:, :]).then_inc(d6, 16)
                    e.wait_ge(d6, 51)
                p6_sync2(e)

            phase(f"p6_{hf}", {"sync": p6_sync, "tensor": p6_tensor,
                               "scalar": p6_scalar_all, "vector": p6_vec})

    emit_ln("p7", rs2_dr, out_d, None, dbg=True)

    if os.environ.get("KDEBUG") == "1":
        dbg_specs = [
            ("d_xt", xt_dr[0, :, 0:512], [128, 512], BF16),
            ("d_sg", sg_dr[0, :, 0:512], [128, 512], BF16),
            ("d_ssm", ssm_fl[0][:, 0:512], [160, 512], F32),
            ("d_y2", y2_in[0][0:128, 0:512], [128, 512], BF16),
            ("d_y2my", y2my[0:128, 0:512], [128, 512], BF16),
            ("d_rs", rs_dr[0:128, 0:512], [128, 512], F32),
            ("d_xnf", xnf_dr[0:128, 0:512], [128, 512], F32),
            ("d_t1", t1_dr[0].rearrange("k p c -> p k c"), [128, 16, 128], BF16),
            ("d_thp", None, [128, 3], F32),
            ("d_rs2", rs2_dr[0:128, 0:512], [128, 512], F32),
            ("d_absi", abs_in[:, :], [1, 8], F32),
            ("d_abso", abs_out[:, :], [1, 8], F32),
            ("d_xnb", xnb_dr[0:128, 0:512], [128, 512], BF16),
            ("d_t2", t2_dr[0][:, 0:512], [128, 512], BF16),
            ("d_g", dbg_g_dr[:, :], [128, 512], BF16),
            ("d_ps", dbg_ps_dr[:, :], [128, 512], F32),
            ("d_s1", None, [128, 1], F32),
            ("d_s23", None, [128, 1], F32),
            ("d_ps2", dbg_ps2_dr[:, :], [128, 512], F32),
            ("d_rs2b", rs2_dr[640:768, 0:512], [128, 512], F32),
        ] + [(f"d_l{nm}", t[:, :], [128, 512], F32) for nm, t in _LND.items()]
        douts = {}
        for nm, ap, shape, dt in dbg_specs:
            douts[nm] = nc.dram_tensor(nm, shape, dt, kind="ExternalOutput")
        dbg_sem = sem("dbg")

        def pdbg_sync(e):
            n = 0
            for nm, ap, shape, dt in dbg_specs:
                if ap is None:
                    continue
                e.dma_start(out=douts[nm].ap(), in_=ap).then_inc(dbg_sem, 16)
                n += 1
            e.dma_start(out=douts["d_thp"][:, :], in_=thp_sb[:, :]).then_inc(dbg_sem, 16)
            e.dma_start(out=douts["d_s1"][:, :], in_=s1_sb[:, :]).then_inc(dbg_sem, 16)
            e.dma_start(out=douts["d_s23"][:, :], in_=s23_sb[:, :]).then_inc(dbg_sem, 16)
            n += 3
            e.wait_ge(dbg_sem, 16 * n)

        phase("pdbg", {"sync": pdbg_sync})

    return nc, stack


_CACHE = {}


def _prep_inputs(inputs):
    f32 = np.float32
    x = np.ascontiguousarray(np.asarray(inputs["L0_attn_hidden_states"], f32)[0].T)
    Winp = np.asarray(inputs["L0_attn_in_proj_w"], f32)
    convw = np.asarray(inputs["L0_attn_conv_w"], f32)
    convb = np.asarray(inputs["L0_attn_conv_b"], f32)
    Wxp = np.asarray(inputs["L0_attn_x_proj_w"], f32)
    Wdt = np.asarray(inputs["L0_attn_dt_proj_w"], f32)
    bdt = np.asarray(inputs["L0_attn_dt_proj_b"], f32)
    A = np.asarray(inputs["L0_attn_A"], f32)
    Dp = np.asarray(inputs["L0_attn_D"], f32)
    Wout = np.asarray(inputs["L0_attn_out_proj_w"], f32)
    w1 = np.asarray(inputs["L0_ffn_w1"], f32)
    w3 = np.asarray(inputs["L0_ffn_w3"], f32)
    w2 = np.asarray(inputs["L0_ffn_w2"], f32)

    def blk_T(w):
        wt = np.zeros((DM, HP), f32)
        wt[:, :H] = w.T
        return np.ascontiguousarray(
            wt.reshape(KDM, 128, NHG, 128).transpose(2, 0, 1, 3))

    w1b = blk_T(w1)
    w3b = blk_T(w3)
    w2t = np.zeros((HP, DM), f32)
    w2t[:H] = w2.T
    w2b = np.ascontiguousarray(w2t.reshape(NHG, 128, DM))

    wot = np.ascontiguousarray(Wout.T)
    w_outb = np.ascontiguousarray(
        wot.reshape(32, 128, 16, 128).transpose(2, 0, 1, 3))

    def parts(w):
        fl = np.ascontiguousarray(w).reshape(-1)
        per = fl.size // NCORES
        return [np.ascontiguousarray(fl[i * per:(i + 1) * per].reshape(128, per // 128))
                for i in range(NCORES)]

    p1s, p3s, p2s = parts(w1), parts(w3), parts(w2)

    in_maps = []
    for c in range(NCORES):
        sl = slice(c * DS, (c + 1) * DS)
        wi = np.ascontiguousarray(
            np.concatenate([Winp[sl], Winp[DI + c * DS:DI + (c + 1) * DS]], 0).T)
        d = {
            "x_all": x,
            "x_res": np.ascontiguousarray(x[:, c * TC:(c + 1) * TC]),
            "w_in": wi,
            "convw": np.ascontiguousarray(convw[sl].reshape(NDG, 128, 4)
                                          .transpose(1, 0, 2).reshape(128, 4 * NDG)),
            "convb": np.ascontiguousarray(convb[sl].reshape(NDG, 128).T),
            "a_mat": np.ascontiguousarray(A[sl].reshape(NDG, 128, S)
                                          .transpose(1, 0, 2).reshape(128, NDG * S)),
            "d_vec": np.ascontiguousarray(Dp[sl].reshape(NDG, 128).T),
            "dt_bv": np.ascontiguousarray(bdt[sl].reshape(NDG, 128).T),
            "w_xp": np.ascontiguousarray(Wxp[:, sl].T),
            "w_dt": np.ascontiguousarray(Wdt[sl].T),
            "w_outb": w_outb,
            "w1b": w1b, "w3b": w3b, "w2b": w2b,
            "w1p": p1s[c], "w3p": p3s[c], "w2p": p2s[c],
            "tok0": np.array([[c * TC]], dtype=np.uint32),
        }
        in_maps.append(d)
    return in_maps


def kernel(**inputs):
    in_maps = _prep_inputs(inputs)
    if "nc" not in _CACHE:
        _CACHE["nc"] = build_nc()
    nc, _stack = _CACHE["nc"]
    res = run_bass_kernel_spmd(nc, in_maps, core_ids=list(range(NCORES)))
    parts = [res.results[c]["out_d"] for c in range(NCORES)]
    full = np.concatenate(parts, axis=1)
    return np.ascontiguousarray(full.T)[None].astype(np.float32)


# version 13 (final)
# speedup vs baseline: 1.0090x; 1.0090x over previous
import contextlib
import os
import numpy as np
import ml_dtypes
import concourse.bass as bass
import concourse.mybir as mybir
from concourse.bass_utils import run_bass_kernel_spmd
from concourse.replica_groups import maybe_share_collective_output_space

F32 = mybir.dt.float32
BF16 = mybir.dt.bfloat16
U32 = mybir.dt.uint32
AF = mybir.ActivationFunctionType
ALU = mybir.AluOpType
BFnp = ml_dtypes.bfloat16

NCORES = 8
RG = [list(range(NCORES))]

T = 16384
DM = 2048
DI = 4096
DS = DI // NCORES
NDG = DS // 128
S = 16
H = 5461
HP = 5504
NHG = HP // 128
TC = T // NCORES
CH = 2048
NCH = T // CH
SW = 2048
NSW = T // SW
NSUB = SW // 512
KDM = DM // 128
WCNT = float(H * DM)


_LND = {}


def build_nc(upto=int(os.environ.get("KUPTO", "99"))):
    nc = bass.Bass("TRN2", target_bir_lowering=False)

    x_all = nc.dram_tensor("x_all", [DM, T], F32, kind="ExternalInput")
    x_res = nc.dram_tensor("x_res", [DM, TC], F32, kind="ExternalInput")
    w_in = nc.dram_tensor("w_in", [DM, 2 * DS], F32, kind="ExternalInput")
    convw = nc.dram_tensor("convw", [128, 4 * NDG], F32, kind="ExternalInput")
    convb = nc.dram_tensor("convb", [128, NDG], F32, kind="ExternalInput")
    a_mat = nc.dram_tensor("a_mat", [128, NDG * S], F32, kind="ExternalInput")
    d_vec = nc.dram_tensor("d_vec", [128, NDG], F32, kind="ExternalInput")
    dt_bv = nc.dram_tensor("dt_bv", [128, NDG], F32, kind="ExternalInput")
    w_xp = nc.dram_tensor("w_xp", [DS, 160], F32, kind="ExternalInput")
    w_dt = nc.dram_tensor("w_dt", [128, DS], F32, kind="ExternalInput")
    w_outb = nc.dram_tensor("w_outb", [16, 32, 128, 128], F32, kind="ExternalInput")
    w1b = nc.dram_tensor("w1b", [NHG, KDM, 128, 128], F32, kind="ExternalInput")
    w3b = nc.dram_tensor("w3b", [NHG, KDM, 128, 128], F32, kind="ExternalInput")
    w2b = nc.dram_tensor("w2b", [NHG, 128, DM], F32, kind="ExternalInput")
    w1p = nc.dram_tensor("w1p", [128, 10922], F32, kind="ExternalInput")
    w3p = nc.dram_tensor("w3p", [128, 10922], F32, kind="ExternalInput")
    w2p = nc.dram_tensor("w2p", [128, 10922], F32, kind="ExternalInput")
    tok0 = nc.dram_tensor("tok0", [1, 1], U32, kind="ExternalInput")

    out_d = nc.dram_tensor("out_d", [DM, TC], F32, kind="ExternalOutput")

    xt_dr = nc.dram_tensor("xt_dr", [NDG, 128, T], BF16, kind="Internal")
    sg_dr = nc.dram_tensor("sg_dr", [NDG, 128, T], BF16, kind="Internal")
    ssm_in = nc.dram_tensor("ssm_in", [4, 160, T // 4], F32, kind="Internal")
    ssm_fl = nc.dram_tensor(
        "ssm_fl", [4, 160, T // 4], F32, kind="Internal",
        addr_space=maybe_share_collective_output_space("AllReduce", RG, prefer_local=True))
    abs_in = nc.dram_tensor("abs_in", [1, 8], F32, kind="Internal")
    abs_out = nc.dram_tensor(
        "abs_out", [1, 8], F32, kind="Internal",
        addr_space=maybe_share_collective_output_space("AllReduce", RG, prefer_local=True))
    t1_dr = nc.dram_tensor("t1_dr", [NHG, KDM, 128, 128], BF16, kind="Internal")
    t3_dr = nc.dram_tensor("t3_dr", [NHG, KDM, 128, 128], BF16, kind="Internal")
    t2_dr = nc.dram_tensor("t2_dr", [NHG, 128, DM], BF16, kind="Internal")
    y2_in = nc.dram_tensor("y2_in", [NCH, DS, CH], BF16, kind="Internal")
    y2_fl = nc.dram_tensor(
        "y2_fl", [NCH, DI, CH], BF16, kind="Internal",
        addr_space=maybe_share_collective_output_space("AllGather", RG, prefer_local=True))
    y2my = nc.dram_tensor("y2my", [DI, TC], BF16, kind="Internal")
    rs_dr = nc.dram_tensor("rs_dr", [DM, TC], F32, kind="Internal")
    xnf_dr = nc.dram_tensor("xnf_dr", [DM, TC], F32, kind="Internal")
    xnb_dr = nc.dram_tensor("xnb_dr", [DM, TC], BF16, kind="Internal")
    rs2_dr = nc.dram_tensor("rs2_dr", [DM, TC], F32, kind="Internal")
    dbg_g_dr = nc.dram_tensor("dbg_g_dr", [128, 512], BF16, kind="Internal")
    dbg_ps_dr = nc.dram_tensor("dbg_ps_dr", [128, 512], F32, kind="Internal")
    dbg_ps2_dr = nc.dram_tensor("dbg_ps2_dr", [128, 512], F32, kind="Internal")
    bc_dr = nc.dram_tensor("bc_dr", [32, T], BF16, kind="Internal")

    stack = contextlib.ExitStack()
    sb = lambda name, shape, dt: stack.enter_context(nc.sbuf_tensor(name, shape, dt))
    sem = lambda name: stack.enter_context(nc.semaphore(name))

    cw_sb = sb("cw_sb", [128, 4 * NDG], F32)
    cb_sb = sb("cb_sb", [128, NDG], F32)
    a_sb = sb("a_sb", [128, NDG * S], F32)
    dv_sb = sb("dv_sb", [128, NDG], F32)
    db_sb = sb("db_sb", [128, NDG], F32)
    xp_sb = sb("xp_sb", [128, NDG * 160], BF16)
    dtw_sb = sb("dtw_sb", [128, DS], BF16)
    onesb_sb = sb("onesb_sb", [128, 1], BF16)
    thp_sb = sb("thp_sb", [128, 3], F32)
    thn_sb = sb("thn_sb", [128, 3], F32)
    s1_sb = sb("s1_sb", [128, 1], F32)
    s23_sb = sb("s23_sb", [128, 1], F32)
    one_sb = sb("one_sb", [128, 1], F32)
    zero_sb = sb("zero_sb", [128, 1], F32)
    eps_sb = sb("eps_sb", [1, 1], F32)

    PH = sem("PH")
    ph_n = [0]

    ph_count = [0]

    def phase(tag, bodies):
        if ph_count[0] >= upto:
            ph_count[0] += 1
            return
        ph_count[0] += 1
        base = ph_n[0]
        with nc.Block(tag) as blk:
            for ename in ("sync", "gpsimd", "tensor", "scalar", "vector"):
                body = bodies.get(ename)

                def make(body=body):
                    def _(eng):
                        eng.wait_ge(PH, base)
                        if body is not None:
                            body(eng)
                        eng.nop().then_inc(PH, 1)
                    return _
                getattr(blk, ename)(make())
        ph_n[0] = base + 5

    ld0 = sem("ld0")

    def p0_sync(e):
        e.dma_start(out=cw_sb[:, :], in_=convw[:, :]).then_inc(ld0, 16)
        e.dma_start(out=cb_sb[:, :], in_=convb[:, :]).then_inc(ld0, 16)
        e.dma_start(out=a_sb[:, :], in_=a_mat[:, :]).then_inc(ld0, 16)
        e.dma_start(out=dv_sb[:, :], in_=d_vec[:, :]).then_inc(ld0, 16)
        e.dma_start(out=db_sb[:, :], in_=dt_bv[:, :]).then_inc(ld0, 16)
        e.wait_ge(ld0, 16 * 7)

    def p0_gp(e):
        e.dma_start(out=xp_sb.rearrange("p (g c) -> p g c", g=NDG),
                    in_=w_xp.rearrange("(g p) c -> p g c", p=128)).then_inc(ld0, 16)
        e.dma_start(out=dtw_sb[:, :], in_=w_dt[:, :]).then_inc(ld0, 16)
        e.wait_ge(ld0, 16 * 7)

    def p0_vec(e):
        e.memset(onesb_sb[:, :], 1.0)
        e.memset(one_sb[:, :], 1.0)
        e.memset(zero_sb[:, :], 0.0)
        e.memset(eps_sb[:, :], 1e-5)

    phase("p0", {"sync": p0_sync, "gpsimd": p0_gp, "vector": p0_vec})

    with contextlib.ExitStack() as _st:
        _en = _st.enter_context
        wp_sb = _en(nc.sbuf_tensor("wp_sb", [128, 10922], F32))
        wd_sb = _en(nc.sbuf_tensor("wd_sb", [128, 10922], F32))
        acol_sb = _en(nc.sbuf_tensor("acol_sb", [128, 3], F32))
        asum_sb = _en(nc.sbuf_tensor("asum_sb", [3, 1], F32))
        onesf_sb = _en(nc.sbuf_tensor("onesf_sb", [128, 1], F32))
        aps = _en(nc.psum_tensor("aps", [3, 1], F32))
        ldb = sem("ldb")
        scb = sem("scb")
        ccb = sem("ccb")
        rdb = sem("rdb")
        def pb_sync(e):
            for i, src in enumerate((w1p, w3p, w2p)):
                if i >= 1:
                    e.wait_ge(scb, i)
                e.dma_start(out=wp_sb[:, :], in_=src[:, :]).then_inc(ldb, 16)

        def pb_scalar(e):
            for i in range(3):
                e.wait_ge(ldb, 16 * (i + 1))
                if i >= 1:
                    e.wait_ge(rdb, i)
                e.activation(wd_sb[:, :], wp_sb[:, :], AF.Abs).then_inc(scb, 1)

        def pb_vec(e):
            e.memset(onesf_sb[:, :], 1.0)
            for i in range(3):
                e.wait_ge(scb, i + 1)
                e.tensor_reduce(acol_sb[:, i:i + 1], wd_sb[:, :],
                                mybir.AxisListType.X, ALU.add).then_inc(rdb, 1)
            e.wait_ge(rdb, 4)
            e.tensor_copy(asum_sb[:, :], aps[:, :]).then_inc(rdb, 1)
            e.wait_ge(ccb, 65)
            e.tensor_scalar(thp_sb[:, 0:3], thp_sb[:, 0:3], 0.5 / WCNT, None,
                            ALU.mult)
            for _ in range(8):
                e.nop()
            e.tensor_scalar(thn_sb[:, 0:3], thp_sb[:, 0:3], -1.0, None, ALU.mult)
            e.tensor_scalar(s1_sb[:, :], thp_sb[:, 0:1], -2.0, None, ALU.mult)
            e.tensor_tensor(s23_sb[:, :], thp_sb[:, 1:2], thp_sb[:, 2:3], ALU.mult)
            for _ in range(8):
                e.nop()
            e.tensor_scalar(s23_sb[:, :], s23_sb[:, :], 4.0, None, ALU.mult)

        def pb_tensor(e):
            e.wait_ge(rdb, 3)
            e.matmul(aps[:, :], acol_sb[:, :], onesf_sb[:, :],
                     start=True, stop=True).then_inc(rdb, 1)

        def pb_gp(e):
            e.wait_ge(rdb, 5)
            e.dma_start(out=bass.AP(abs_in, 0, [[1, 1], [1, 3]]),
                        in_=asum_sb[:, :]).then_inc(ccb, 16)
            e.wait_ge(ccb, 16)
            e.collective_compute(
                "AllReduce", ALU.add, replica_groups=RG,
                ins=[abs_in[:, :]], outs=[abs_out[:, :]]).then_inc(ccb, 1)
            e.wait_ge(ccb, 17)
            for j in range(3):
                e.dma_start(out=thp_sb[:, j:j + 1],
                            in_=bass.AP(abs_out, j, [[0, 128], [1, 1]])
                            ).then_inc(ccb, 16)
            e.wait_ge(ccb, 65)

        phase("p0b", {"sync": pb_sync, "scalar": pb_scalar, "tensor": pb_tensor,
                      "gpsimd": pb_gp, "vector": pb_vec})

    with contextlib.ExitStack() as _st:
        _en = _st.enter_context
        wi_sb = _en(nc.sbuf_tensor("wi_sb", [128, KDM * 1024], BF16))
        xb_sb = _en(nc.sbuf_tensor("xb_sb", [128, KDM * SW], BF16))
        cv_sb = _en(nc.sbuf_tensor("cv_sb", [128, NDG * (SW + 3)], F32))
        cvac_sb = _en(nc.sbuf_tensor("cvac_sb", [128, SW], F32))
        xtw_sb = _en(nc.sbuf_tensor("xtw_sb", [128, NDG * SW], BF16))
        sgw_sb = _en(nc.sbuf_tensor("sgw_sb", [128, NDG * SW], BF16))
        sse_sb = _en(nc.sbuf_tensor("sse_sb", [128, 512], F32))
        ssb_sb = _en(nc.sbuf_tensor("ssb_sb", [32, 512], F32))
        ip_ps0 = _en(nc.psum_tensor("ip_ps0", [128, 512], F32))
        ip_ps1 = _en(nc.psum_tensor("ip_ps1", [128, 512], F32))
        st_ps = _en(nc.psum_tensor("st_ps", [128, 512], F32))
        sb_ps = _en(nc.psum_tensor("sb_ps", [32, 512], F32))
        l1w = sem("l1w")
        l1xp = [sem("l1xa"), sem("l1xb")]
        m1 = sem("m1")
        e1 = sem("e1")
        v1 = sem("v1")
        x1 = sem("x1")
        sm1 = sem("sm1")
        se1 = sem("se1")
        st_xt = sem("st_xt")
        st_sg = sem("st_sg")
        stq = [sem(f"stq{i}") for i in range(4)]
        cc1 = sem("cc1")
        def p1_gp(e):
            e.dma_start(out=wi_sb.rearrange("p (k e) -> p k e", k=KDM),
                        in_=w_in.rearrange("(k p) e -> p k e", p=128)).then_inc(l1w, 16)
            for li in range(2 * NSW):
                if li >= 2:
                    e.wait_ge(m1, 16 * (li - 1))
                pr = li % 2
                e.dma_start(
                    out=xb_sb[:, pr * KDM * 1024:(pr + 1) * KDM * 1024]
                    .rearrange("p (k t) -> p k t", k=KDM),
                    in_=x_all.rearrange("(k p) t -> p k t", p=128)[
                        :, :, li * 1024:(li + 1) * 1024]).then_inc(l1xp[li % 2], 16)
            for q in range(4):
                e.wait_ge(stq[q], 16 * 16)
                e.collective_compute(
                    "AllReduce", ALU.add, replica_groups=RG,
                    ins=[ssm_in[q]], outs=[ssm_fl[q]]).then_inc(cc1, 1)
            e.wait_ge(cc1, 4)

        def p1_tensor(e):
            e.wait_ge(l1w, 16)
            for sw in range(NSW):
                for sub in range(NSUB):
                    li = sw * 2 + sub // 2
                    pr = li % 2
                    e.wait_ge(l1xp[pr], 16 * (li // 2 + 1))
                    for eg in range(8):
                        mg = sw * 32 + sub * 8 + eg
                        ps = ip_ps0 if mg % 2 == 0 else ip_ps1
                        if mg >= 2:
                            e.wait_ge(e1, mg - 1)
                        xoff = pr * KDM * 1024 + (sub % 2) * 512
                        for k in range(KDM):
                            e.matmul(
                                ps[:, :],
                                wi_sb[:, k * 1024 + eg * 128:k * 1024 + (eg + 1) * 128],
                                xb_sb[:, xoff + k * 1024:xoff + k * 1024 + 512],
                                start=(k == 0), stop=(k == KDM - 1),
                            ).then_inc(m1, 1 if k == KDM - 1 else 0)
                e.wait_ge(x1, (sw + 1) * NDG)
                for sub in range(NSUB):
                    gidx = sw * NSUB + sub
                    if gidx >= 1:
                        e.wait_ge(se1, gidx)
                    for dg in range(NDG):
                        xt_ap = xtw_sb[:, dg * SW + sub * 512:dg * SW + (sub + 1) * 512]
                        e.matmul(st_ps[:, :], xp_sb[:, dg * 160:dg * 160 + 128],
                                 xt_ap, start=(dg == 0), stop=(dg == NDG - 1))
                        e.matmul(sb_ps[:, :], xp_sb[:, dg * 160 + 128:(dg + 1) * 160],
                                 xt_ap, start=(dg == 0), stop=(dg == NDG - 1)
                                 ).then_inc(sm1, 1 if dg == NDG - 1 else 0)

        def p1_scalar(e):
            for sw in range(NSW):
                for sub in range(NSUB):
                    for eg in range(8):
                        mg = sw * 32 + sub * 8 + eg
                        ps = ip_ps0 if mg % 2 == 0 else ip_ps1
                        e.wait_ge(m1, mg + 1)
                        if eg < 4:
                            dg = eg
                            if sw >= 1 and sub == 0:
                                e.wait_ge(v1, sw * NDG)
                            e.copy(
                                cv_sb[:, dg * (SW + 3) + 3 + sub * 512:
                                      dg * (SW + 3) + 3 + (sub + 1) * 512],
                                ps[:, :]).then_inc(e1, 1)
                        else:
                            dg = eg - 4
                            if sw >= 1 and sub == 0:
                                e.wait_ge(st_sg, 16 * 4 * sw)
                            e.activation(
                                sgw_sb[:, dg * SW + sub * 512:dg * SW + (sub + 1) * 512],
                                ps[:, :], AF.Silu).then_inc(e1, 1)
                for dg in range(NDG):
                    e.wait_ge(v1, sw * NDG + dg + 1)
                    if sw >= 1:
                        e.wait_ge(sm1, sw * NSUB)
                        e.wait_ge(st_xt, 16 * 4 * sw)
                    e.activation(
                        xtw_sb[:, dg * SW:(dg + 1) * SW],
                        cv_sb[:, dg * (SW + 3) + 3:dg * (SW + 3) + 3 + SW],
                        AF.Silu, bias=cb_sb[:, dg:dg + 1]).then_inc(x1, 1)
                for sub in range(NSUB):
                    gidx = sw * NSUB + sub
                    e.wait_ge(sm1, gidx + 1)
                    if gidx >= 1:
                        pq = (gidx - 1) // 8
                        e.wait_ge(stq[pq], 16 * 2 * ((gidx - 1) % 8 + 1))
                    e.copy(sse_sb[:, :], st_ps[:, :])
                    e.copy(ssb_sb[:, :], sb_ps[:, :]).then_inc(se1, 1)

        def p1_vector(e):
            for sw in range(NSW):
                for dg in range(NDG):
                    base = dg * (SW + 3)
                    if sw == 0:
                        e.memset(cv_sb[:, base:base + 3], 0.0)
                    e.wait_ge(e1, sw * 32 + 3 * 8 + dg + 1)
                    e.tensor_scalar(cvac_sb[:, :], cv_sb[:, base:base + SW],
                                    cw_sb[:, dg * 4:dg * 4 + 1], None, ALU.mult)
                    for j in (1, 2, 3):
                        e.scalar_tensor_tensor(
                            cvac_sb[:, :], cv_sb[:, base + j:base + j + SW],
                            cw_sb[:, dg * 4 + j:dg * 4 + j + 1], cvac_sb[:, :],
                            ALU.mult, ALU.add)
                    e.tensor_copy(cv_sb[:, base:base + 3],
                                  cv_sb[:, base + SW:base + SW + 3])
                    e.tensor_copy(cv_sb[:, base + 3:base + 3 + SW],
                                  cvac_sb[:, :]).then_inc(v1, 1)

        def p1_sync(e):
            for sw in range(NSW):
                for dg in range(NDG):
                    e.wait_ge(x1, sw * NDG + dg + 1)
                    e.dma_start(out=xt_dr[dg, :, sw * SW:(sw + 1) * SW],
                                in_=xtw_sb[:, dg * SW:(dg + 1) * SW]).then_inc(st_xt, 16)
                for dg in range(NDG):
                    e.wait_ge(e1, sw * 32 + 32)
                    e.dma_start(out=sg_dr[dg, :, sw * SW:(sw + 1) * SW],
                                in_=sgw_sb[:, dg * SW:(dg + 1) * SW]).then_inc(st_sg, 16)
                for sub in range(NSUB):
                    gidx = sw * NSUB + sub
                    e.wait_ge(se1, gidx + 1)
                    t0 = sw * SW + sub * 512
                    q, qo = divmod(t0, T // 4)
                    e.dma_start(out=ssm_in[q, 0:128, qo:qo + 512],
                                in_=sse_sb[:, :]).then_inc(stq[q], 16)
                    e.dma_start(out=ssm_in[q, 128:160, qo:qo + 512],
                                in_=ssb_sb[:, :]).then_inc(stq[q], 16)
            e.wait_ge(st_xt, 16 * 4 * NSW)
            e.wait_ge(st_sg, 16 * 4 * NSW)
            for q in range(4):
                e.wait_ge(stq[q], 16 * 16)

        phase("p1", {"gpsimd": p1_gp, "tensor": p1_tensor, "scalar": p1_scalar,
                     "vector": p1_vector, "sync": p1_sync})

    qjobs_all = ([(w1b[hg].rearrange("k p c -> p k c"),
                   t1_dr[hg].rearrange("k p c -> p k c"), 0) for hg in range(NHG)]
                 + [(w3b[hg].rearrange("k p c -> p k c"),
                     t3_dr[hg].rearrange("k p c -> p k c"), 1) for hg in range(NHG)]
                 + [(w2b[hg][:, :], t2_dr[hg][:, :], 2) for hg in range(NHG)])

    with contextlib.ExitStack() as _st:
        _en = _st.enter_context
        dt_sb = _en(nc.sbuf_tensor("dt_sb", [128, NDG * CH], F32))
        xtc_sb = _en(nc.sbuf_tensor("xtc_sb", [128, NDG * CH], BF16))
        sgc_sb = _en(nc.sbuf_tensor("sgc_sb", [128, NDG * CH], BF16))
        wv_sb = _en(nc.sbuf_tensor("wv_sb", [128, NDG * CH], BF16))
        ts_sb = _en(nc.sbuf_tensor("ts_sb", [128, CH * 2], BF16))
        bbc_sb = _en(nc.sbuf_tensor("bbc_sb", [128, CH * 2], BF16))
        cbc_sb = _en(nc.sbuf_tensor("cbc_sb", [128, CH * 2], BF16))
        da_sb = _en(nc.sbuf_tensor("da_sb", [128, CH * 2], F32))
        dbx_sb = _en(nc.sbuf_tensor("dbx_sb", [128, CH], BF16))
        so_sb = _en(nc.sbuf_tensor("so_sb", [128, CH], BF16))
        tmp_sb = _en(nc.sbuf_tensor("tmp_sb", [128, CH * 4], BF16))
        ya_sb = _en(nc.sbuf_tensor("ya_sb", [128, NDG * CH], F32))
        stc_sb = _en(nc.sbuf_tensor("stc_sb", [128, NDG * S], F32))
        y2_sb = _en(nc.sbuf_tensor("y2_sb", [128, NDG * CH], BF16))
        spt_sb = _en(nc.sbuf_tensor("spt_sb", [128, 512], F32))
        dt_ps0 = _en(nc.psum_tensor("dt_ps0", [128, 512], F32))
        dt_ps1 = _en(nc.psum_tensor("dt_ps1", [128, 512], F32))
        l3g = sem("l3g")
        l3s = sem("l3s")
        m3 = sem("m3")
        sp3 = sem("sp3")
        vw3 = sem("vw3")
        bc3 = sem("bc3")
        ex3 = sem("ex3")
        vs3 = sem("vs3")
        acc = [sem(f"acc{i}") for i in range(4)]
        vt3 = sem("vt3")
        g3 = sem("g3")
        vy3 = sem("vy3")
        st3c = [sem(f"st3c{i}") for i in range(NCH)]
        l3b = [sem(f"l3b{i}") for i in range(NCH)]
        cc3 = sem("cc3")
        NSUBC = CH // 512

        def p3_gp(e):
            for cb in range(NCH):
                qb, qob = divmod(cb * CH, T // 4)
                e.dma_start(out=bc_dr[:, cb * CH:(cb + 1) * CH],
                            in_=ssm_fl[qb, 128:160, qob:qob + CH]
                            ).then_inc(l3b[cb], 16)
            e.dma_start(out=ts_sb[:, 0:CH],
                        in_=ssm_fl[0, 0:128, 0:CH]).then_inc(l3g, 16)
            for c in range(NCH):
                for s in range(S):
                    for dg in range(NDG):
                        it = (c * S + s) * NDG + dg
                        pr = it % 2
                        if s == 8 and dg == 0 and c + 1 < NCH:
                            cn = c + 1
                            if cn >= 2:
                                e.wait_ge(m3, NDG * NSUBC * (cn - 1))
                            q, qo = divmod(cn * CH, T // 4)
                            e.dma_start(
                                out=ts_sb[:, (cn % 2) * CH:(cn % 2) * CH + CH],
                                in_=ssm_fl[q, 0:128, qo:qo + CH]
                                ).then_inc(l3g, 16)
                        e.wait_ge(vt3, it + 1)
                        prt = it % 4
                        if s == 0:
                            e.dma_start(
                                out=ya_sb[:, dg * CH:(dg + 1) * CH],
                                in_=tmp_sb[:, prt * CH:(prt + 1) * CH]
                                ).then_inc(acc[prt], 16)
                        else:
                            e.dma_start(
                                out=ya_sb[:, dg * CH:(dg + 1) * CH],
                                in_=tmp_sb[:, prt * CH:(prt + 1) * CH],
                                accum_op=ALU.add).then_inc(acc[prt], 16)
                e.wait_ge(st3c[c], 16 * NDG)
                e.collective_compute(
                    "AllGather", ALU.bypass, replica_groups=RG,
                    ins=[y2_in[c]], outs=[y2_fl[c]]).then_inc(cc3, 1)
            e.wait_ge(cc3, NCH)

        def p3_sync(e):
            for c in range(NCH):
                if c >= 1:
                    e.wait_ge(vy3, 2 * NDG * c)
                for dg in range(NDG):
                    e.dma_start(out=xtc_sb[:, dg * CH:(dg + 1) * CH],
                                in_=xt_dr[dg, :, c * CH:(c + 1) * CH]).then_inc(l3s, 16)
                    e.dma_start(out=sgc_sb[:, dg * CH:(dg + 1) * CH],
                                in_=sg_dr[dg, :, c * CH:(c + 1) * CH]).then_inc(l3s, 16)
                e.wait_ge(l3b[c], 16)
                for s in range(S):
                    gs = c * S + s
                    if gs >= 2:
                        e.wait_ge(vt3, (gs - 1) * NDG)
                    pr = s % 2
                    e.dma_start(
                        out=bbc_sb[:, pr * CH:(pr + 1) * CH],
                        in_=bass.AP(bc_dr, s * T + c * CH, [[0, 128], [1, CH]])
                        ).then_inc(bc3, 16)
                    e.dma_start(
                        out=cbc_sb[:, pr * CH:(pr + 1) * CH],
                        in_=bass.AP(bc_dr, (S + s) * T + c * CH, [[0, 128], [1, CH]])
                        ).then_inc(bc3, 16)
                for dg in range(NDG):
                    e.wait_ge(vy3, 2 * (c * NDG + dg + 1))
                    e.dma_start(
                        out=y2_in[c, dg * 128:(dg + 1) * 128, :],
                        in_=y2_sb[:, dg * CH:(dg + 1) * CH]).then_inc(st3c[c], 16)

        def p3_tensor(e):
            for c in range(NCH):
                e.wait_ge(l3g, 16 * (c + 1))
                for dg in range(NDG):
                    for sub in range(NSUBC):
                        gi = (c * NDG + dg) * NSUBC + sub
                        ps = dt_ps0 if gi % 2 == 0 else dt_ps1
                        if gi >= 2:
                            e.wait_ge(sp3, 2 * (gi - 1))
                        e.matmul(ps[:, :], dtw_sb[:, dg * 128:(dg + 1) * 128],
                                 ts_sb[:, (c % 2) * CH + sub * 512:
                                        (c % 2) * CH + (sub + 1) * 512],
                                 start=True, stop=True).then_inc(m3, 1)

        def p3_scalar(e):
            for c in range(NCH):
                for dg in range(NDG):
                    if c >= 1:
                        e.wait_ge(vw3, (c - 1) * NDG + dg + 1)
                    for sub in range(NSUBC):
                        gi = (c * NDG + dg) * NSUBC + sub
                        ps = dt_ps0 if gi % 2 == 0 else dt_ps1
                        e.wait_ge(m3, gi + 1)
                        e.activation(spt_sb[:, :], ps[:, :], AF.Exp,
                                     bias=db_sb[:, dg:dg + 1]).then_inc(sp3, 1)
                        e.activation(
                            dt_sb[:, dg * CH + sub * 512:dg * CH + (sub + 1) * 512],
                            spt_sb[:, :], AF.Ln, bias=one_sb[:, :]).then_inc(sp3, 1)
                for s in range(S):
                    for dg in range(NDG):
                        it = (c * S + s) * NDG + dg
                        pr = it % 2
                        if it >= 2:
                            e.wait_ge(vs3, it - 1)
                        e.activation(
                            da_sb[:, pr * CH:(pr + 1) * CH],
                            dt_sb[:, dg * CH:(dg + 1) * CH], AF.Exp,
                            scale=a_sb[:, dg * S + s:dg * S + s + 1]).then_inc(ex3, 1)

        def p3_vector(e):
            e.memset(stc_sb[:, :], 0.0)
            for c in range(NCH):
                e.wait_ge(l3s, 16 * 8 * (c + 1))
                for dg in range(NDG):
                    e.wait_ge(sp3, ((c * NDG + dg) * NSUBC + NSUBC) * 2)
                    e.tensor_tensor(wv_sb[:, dg * CH:(dg + 1) * CH],
                                    dt_sb[:, dg * CH:(dg + 1) * CH],
                                    xtc_sb[:, dg * CH:(dg + 1) * CH],
                                    ALU.mult).then_inc(vw3, 1)
                for s in range(S):
                    e.wait_ge(bc3, 32 * (c * S + s + 1))
                    prs = s % 2
                    for dg in range(NDG):
                        it = (c * S + s) * NDG + dg
                        pr = it % 2
                        e.tensor_tensor(dbx_sb[:, :], wv_sb[:, dg * CH:(dg + 1) * CH],
                                        bbc_sb[:, prs * CH:(prs + 1) * CH], ALU.mult)
                        e.wait_ge(ex3, it + 1)
                        col = dg * S + s
                        e.tensor_tensor_scan(
                            so_sb[:, :], da_sb[:, pr * CH:(pr + 1) * CH],
                            dbx_sb[:, :], stc_sb[:, col:col + 1],
                            ALU.mult, ALU.add).then_inc(vs3, 1)
                        prt = it % 4
                        if it >= 4:
                            e.wait_ge(acc[prt], 16 * (it // 4))
                        e.tensor_tensor(
                            tmp_sb[:, prt * CH:(prt + 1) * CH], so_sb[:, :],
                            cbc_sb[:, prs * CH:(prs + 1) * CH], ALU.mult).then_inc(vt3, 1)
                        e.tensor_copy(stc_sb[:, col:col + 1], so_sb[:, CH - 1:CH])
                if c >= 1:
                    e.wait_ge(st3c[c - 1], 16 * NDG)
                for pa in range(4):
                    e.wait_ge(acc[pa], 16 * 16 * (c + 1))
                for dg in range(NDG):
                    e.scalar_tensor_tensor(
                        ya_sb[:, dg * CH:(dg + 1) * CH],
                        xtc_sb[:, dg * CH:(dg + 1) * CH],
                        dv_sb[:, dg:dg + 1],
                        ya_sb[:, dg * CH:(dg + 1) * CH],
                        ALU.mult, ALU.add).then_inc(vy3, 1)
                    e.tensor_tensor(
                        y2_sb[:, dg * CH:(dg + 1) * CH],
                        ya_sb[:, dg * CH:(dg + 1) * CH],
                        sgc_sb[:, dg * CH:(dg + 1) * CH], ALU.mult).then_inc(vy3, 1)

        phase("p3", {"gpsimd": p3_gp, "sync": p3_sync, "tensor": p3_tensor,
                     "scalar": p3_scalar, "vector": p3_vector})

    with contextlib.ExitStack() as _st:
        _en = _st.enter_context
        tk_sb = _en(nc.sbuf_tensor("tk_sb", [1, 1], U32))
        l4 = sem("l4")
        c4 = sem("c4")
        def p4_gp(e):
            e.dma_start(out=tk_sb[:, :], in_=tok0[:, :]).then_inc(l4, 16)
            e.wait_ge(l4, 16)
            with e.register("p4off") as roff, e.register("p4r0") as r0, \
                    e.register("p4r") as r4:
                e.load(roff, tk_sb[0:1, 0:1])
                e.reg_mul(r0, roff, DI)
                for j in range(8):
                    e.reg_add(r4, r0, j * 512 * CH)
                    e.dma_start(
                        out=y2my[j * 512:(j + 1) * 512, :],
                        in_=bass.AP(y2_fl, r4, [[CH, 512], [1, CH]]),
                    ).then_inc(c4, 16)
            e.wait_ge(c4, 16 * 8)

        phase("p4", {"gpsimd": p4_gp})

    with contextlib.ExitStack() as _st:
        _en = _st.enter_context
        yf_sb = _en(nc.sbuf_tensor("yf_sb", [128, 32 * TC], BF16))
        wo_sb = _en(nc.sbuf_tensor("wo_sb", [128, 2 * 32 * 128], BF16))
        xr_sb = _en(nc.sbuf_tensor("xr_sb", [128, 2 * 512], F32))
        rs_sb = _en(nc.sbuf_tensor("rs_sb", [128, 2 * 512], F32))
        op_ps0 = _en(nc.psum_tensor("op_ps0", [128, 512], F32))
        op_ps1 = _en(nc.psum_tensor("op_ps1", [128, 512], F32))
        l5y = sem("l5y")
        l5w = sem("l5w")
        l5x = sem("l5x")
        m5 = sem("m5")
        v5 = sem("v5")
        st5 = sem("st5")
        def p5_sync(e):
            for k in range(32):
                e.dma_start(out=yf_sb[:, k * TC:(k + 1) * TC],
                            in_=y2my[k * 128:(k + 1) * 128, :]).then_inc(l5y, 16)
            for gi in range(64):
                eg, ts_i = divmod(gi, 4)
                pr = gi % 2
                if gi >= 2:
                    e.wait_ge(v5, gi - 1)
                e.dma_start(
                    out=xr_sb[:, pr * 512:(pr + 1) * 512],
                    in_=x_res[eg * 128:(eg + 1) * 128, ts_i * 512:(ts_i + 1) * 512],
                ).then_inc(l5x, 16)
        def p5_st(e):
            for gi in range(64):
                eg, ts_i = divmod(gi, 4)
                pr = gi % 2
                e.wait_ge(v5, gi + 1)
                e.dma_start(
                    out=rs_dr[eg * 128:(eg + 1) * 128, ts_i * 512:(ts_i + 1) * 512],
                    in_=rs_sb[:, pr * 512:(pr + 1) * 512]).then_inc(st5, 16)
            e.wait_ge(st5, 16 * 64)

        def p5_gp(e):
            for eg in range(16):
                pr = eg % 2
                if eg >= 2:
                    e.wait_ge(m5, (eg - 1) * 4)
                e.dma_start(
                    out=wo_sb[:, pr * 4096:(pr + 1) * 4096].rearrange(
                        "p (k c) -> p k c", k=32),
                    in_=w_outb[eg].rearrange("k p c -> p k c")).then_inc(l5w, 16)

        def p5_tensor(e):
            e.wait_ge(l5y, 16 * 32)
            for eg in range(16):
                pr = eg % 2
                e.wait_ge(l5w, 16 * (eg + 1))
                for ts_i in range(4):
                    gi = eg * 4 + ts_i
                    ps = op_ps0 if gi % 2 == 0 else op_ps1
                    if gi >= 2:
                        e.wait_ge(v5, gi - 1)
                    for k in range(32):
                        e.matmul(
                            ps[:, :],
                            wo_sb[:, (pr * 32 + k) * 128:(pr * 32 + k + 1) * 128],
                            yf_sb[:, k * TC + ts_i * 512:k * TC + (ts_i + 1) * 512],
                            start=(k == 0), stop=(k == 31),
                        ).then_inc(m5, 1 if k == 31 else 0)

        def p5_vec(e):
            for gi in range(64):
                ps = op_ps0 if gi % 2 == 0 else op_ps1
                pr = gi % 2
                e.wait_ge(m5, gi + 1)
                e.wait_ge(l5x, 16 * (gi + 1))
                if gi >= 2:
                    e.wait_ge(st5, 16 * (gi - 1))
                e.tensor_tensor(rs_sb[:, pr * 512:(pr + 1) * 512], ps[:, :],
                                xr_sb[:, pr * 512:(pr + 1) * 512],
                                ALU.add).then_inc(v5, 1)

        phase("p5a", {"sync": p5_sync, "gpsimd": p5_gp, "tensor": p5_tensor,
                      "vector": p5_vec, "scalar": p5_st})

    def emit_ln(tag, src_dr, dst_f_dr, dst_b_dr, dbg=False, qjobs=None):
        n_ts, tw = 4, 512
        if qjobs:
            NJ = len(qjobs)
            qsched = [[] for _ in range(n_ts)]
            for j in range(NJ):
                qsched[min(j * n_ts // NJ, n_ts - 1)].append(j)
        rows_dr = nc.dram_tensor(tag + "rows", [2, tw], F32, kind="Internal")
        dbg = dbg and os.environ.get("KDEBUG") == "1"
        if dbg:
            lnd = {nm: nc.dram_tensor(tag + "lnd" + nm, [128, tw], F32,
                                      kind="Internal")
                   for nm in ("rt", "mb", "rs", "xo")}
            _LND.update(lnd)
            oD = sem(tag + "oD")
        VTOT = 18
        STOT = 33 + (16 if dst_b_dr is not None else 0)
        NOUT = 32 if dst_b_dr is not None else 16
        with contextlib.ExitStack() as _st:
            _en = _st.enter_context
            rt_sb = _en(nc.sbuf_tensor(tag + "rt", [128, 2 * KDM * tw], F32))
            rb_sb = _en(nc.sbuf_tensor(tag + "rb", [128, KDM * tw], BF16))
            sq_sb = _en(nc.sbuf_tensor(tag + "sq", [128, KDM * tw], BF16))
            mr_sb = _en(nc.sbuf_tensor(tag + "mr", [1, tw], F32))
            qr_sb = _en(nc.sbuf_tensor(tag + "qr", [1, tw], F32))
            m2_sb = _en(nc.sbuf_tensor(tag + "m2", [1, tw], F32))
            mb_sb = _en(nc.sbuf_tensor(tag + "mb", [128, tw], F32))
            rstb_sb = _en(nc.sbuf_tensor(tag + "rs", [128, tw], F32))
            xo_sb = _en(nc.sbuf_tensor(tag + "xo", [128, KDM * tw], F32))
            xb2_sb = _en(nc.sbuf_tensor(tag + "xb", [128, KDM * tw], BF16))
            sx_ps = _en(nc.psum_tensor(tag + "px", [1, tw], F32))
            sq_ps = _en(nc.psum_tensor(tag + "pq", [1, tw], F32))
            if qjobs:
                qw_sb = _en(nc.sbuf_tensor(tag + "qw", [128, 2048 * 2], F32))
                qp_sb = _en(nc.sbuf_tensor(tag + "qp", [128, 2048], BF16))
                qt_sb = _en(nc.sbuf_tensor(tag + "qt", [128, 2048 * 2], BF16))
                l2p = [sem(tag + "l2a"), sem(tag + "l2b")]
                v2 = sem(tag + "v2")
                s2p = [sem(tag + "s2a"), sem(tag + "s2b")]
            lA = sem(tag + "l")
            sA = sem(tag + "s")
            mA = sem(tag + "m")
            vA = sem(tag + "v")
            bA = sem(tag + "b")
            oA = sem(tag + "o")
            def b_sync(e):
                for ts_i in range(n_ts):
                    pr = ts_i % 2
                    if ts_i >= 2:
                        e.wait_ge(vA, (ts_i - 1) * VTOT)
                        e.wait_ge(sA, (ts_i - 1) * STOT + 32)
                    if dbg and ts_i == 1:
                        e.wait_ge(oD, 64)
                    for k in range(KDM):
                        e.dma_start(
                            out=rt_sb[:, (pr * KDM + k) * tw:(pr * KDM + k + 1) * tw],
                            in_=src_dr[k * 128:(k + 1) * 128,
                                       ts_i * tw:(ts_i + 1) * tw]).then_inc(lA, 16)
                    e.wait_ge(vA, ts_i * VTOT + 2)
                    e.dma_start(out=rows_dr[0:1, :], in_=mr_sb[:, :]).then_inc(bA, 16)
                    e.dma_start(out=rows_dr[1:2, :], in_=qr_sb[:, :]).then_inc(bA, 16)
                    e.wait_ge(bA, 64 * ts_i + 32)
                    e.dma_start(out=mb_sb[:, :],
                                in_=bass.AP(rows_dr, 0, [[0, 128], [1, tw]])
                                ).then_inc(bA, 16)
                    e.dma_start(out=rstb_sb[:, :],
                                in_=bass.AP(rows_dr, tw, [[0, 128], [1, tw]])
                                ).then_inc(bA, 16)
                    for k in range(KDM):
                        e.wait_ge(vA, ts_i * VTOT + 2 + k + 1)
                        e.dma_start(
                            out=dst_f_dr[k * 128:(k + 1) * 128,
                                         ts_i * tw:(ts_i + 1) * tw],
                            in_=xo_sb[:, k * tw:(k + 1) * tw]).then_inc(oA, 16)
                        if dst_b_dr is not None:
                            e.wait_ge(sA, ts_i * STOT + 33 + k + 1)
                            e.dma_start(
                                out=dst_b_dr[k * 128:(k + 1) * 128,
                                             ts_i * tw:(ts_i + 1) * tw],
                                in_=xb2_sb[:, k * tw:(k + 1) * tw]).then_inc(oA, 16)
                    if dbg and ts_i == 0:
                        e.wait_ge(vA, VTOT)
                        e.dma_start(out=lnd["rt"][:, :],
                                    in_=rt_sb[:, 0:tw]).then_inc(oD, 16)
                        e.dma_start(out=lnd["mb"][:, :],
                                    in_=mb_sb[:, :]).then_inc(oD, 16)
                        e.dma_start(out=lnd["rs"][:, :],
                                    in_=rstb_sb[:, :]).then_inc(oD, 16)
                        e.dma_start(out=lnd["xo"][:, :],
                                    in_=xo_sb[:, 0:tw]).then_inc(oD, 16)
                e.wait_ge(oA, 16 * NOUT * n_ts)
                if dbg:
                    e.wait_ge(oD, 64)

            def b_scalar(e):
                for ts_i in range(n_ts):
                    pr = ts_i % 2
                    for k in range(KDM):
                        e.wait_ge(lA, 16 * KDM * (ts_i + 1))
                        if ts_i >= 1:
                            e.wait_ge(mA, ts_i)
                        e.copy(rb_sb[:, k * tw:(k + 1) * tw],
                               rt_sb[:, (pr * KDM + k) * tw:(pr * KDM + k + 1) * tw]
                               ).then_inc(sA, 1)
                        e.activation(sq_sb[:, k * tw:(k + 1) * tw],
                                     rt_sb[:, (pr * KDM + k) * tw:(pr * KDM + k + 1) * tw],
                                     AF.Square).then_inc(sA, 1)
                    e.wait_ge(vA, ts_i * VTOT + 1)
                    e.activation(qr_sb[:, :], qr_sb[:, :], AF.Sqrt,
                                 bias=eps_sb[:, :]).then_inc(sA, 1)
                    if dst_b_dr is not None:
                        for k in range(KDM):
                            e.wait_ge(vA, ts_i * VTOT + 2 + k + 1)
                            e.copy(xb2_sb[:, k * tw:(k + 1) * tw],
                                   xo_sb[:, k * tw:(k + 1) * tw]).then_inc(sA, 1)

            def b_tensor(e):
                for ts_i in range(n_ts):
                    e.wait_ge(sA, ts_i * STOT + 32)
                    if ts_i >= 1:
                        e.wait_ge(vA, (ts_i - 1) * VTOT + 1)
                    for k in range(KDM):
                        e.matmul(sx_ps[:, :], onesb_sb[:, :],
                                 rb_sb[:, k * tw:(k + 1) * tw],
                                 start=(k == 0), stop=(k == KDM - 1))
                        e.matmul(sq_ps[:, :], onesb_sb[:, :],
                                 sq_sb[:, k * tw:(k + 1) * tw],
                                 start=(k == 0), stop=(k == KDM - 1)
                                 ).then_inc(mA, 1 if k == KDM - 1 else 0)

            def b_vec(e):
                for ts_i in range(n_ts):
                    e.wait_ge(mA, ts_i + 1)
                    if ts_i >= 1:
                        e.wait_ge(bA, 64 * (ts_i - 1) + 32)
                    e.tensor_scalar(mr_sb[:, :], sx_ps[:, :], 1.0 / DM, None,
                                    ALU.mult)
                    e.tensor_scalar(qr_sb[:, :], sq_ps[:, :], 1.0 / DM, None,
                                    ALU.mult)
                    for _ in range(4):
                        e.nop()
                    e.tensor_tensor(m2_sb[:, :], mr_sb[:, :], mr_sb[:, :], ALU.mult)
                    for _ in range(4):
                        e.nop()
                    e.tensor_tensor(qr_sb[:, :], qr_sb[:, :], m2_sb[:, :],
                                    ALU.subtract).then_inc(vA, 1)
                    e.wait_ge(sA, ts_i * STOT + 33)
                    e.reciprocal(qr_sb[:, :], qr_sb[:, :]).then_inc(vA, 1)
                    e.wait_ge(bA, 64 * (ts_i + 1))
                    if ts_i >= 1:
                        e.wait_ge(oA, 16 * NOUT * ts_i)
                    for k in range(KDM):
                        e.tensor_tensor(
                            xo_sb[:, k * tw:(k + 1) * tw],
                            rt_sb[:, ((ts_i % 2) * KDM + k) * tw:
                                  ((ts_i % 2) * KDM + k + 1) * tw],
                            mb_sb[:, :], ALU.subtract)
                    for k in range(KDM):
                        e.tensor_tensor(xo_sb[:, k * tw:(k + 1) * tw],
                                        xo_sb[:, k * tw:(k + 1) * tw],
                                        rstb_sb[:, :], ALU.mult).then_inc(vA, 1)
                    if qjobs:
                        for j in qsched[ts_i]:
                            half = (j % 2) * 2048
                            e.wait_ge(l2p[j % 2], 16 * (j // 2 + 1))
                            if j >= 2:
                                e.wait_ge(s2p[j % 2], 16 * (j // 2))
                            e.tensor_scalar(qp_sb[:, :], qw_sb[:, half:half + 2048],
                                            thp_sb[:, qjobs[j][2]:qjobs[j][2] + 1],
                                            None, ALU.is_gt)
                            e.scalar_tensor_tensor(
                                qt_sb[:, half:half + 2048],
                                qw_sb[:, half:half + 2048],
                                thn_sb[:, qjobs[j][2]:qjobs[j][2] + 1], qp_sb[:, :],
                                ALU.is_lt, ALU.subtract).then_inc(v2, 1)

            def b_gp(e):
                if not qjobs:
                    return
                NJq = len(qjobs)

                def q_store(j):
                    half = (j % 2) * 2048
                    e.wait_ge(v2, j + 1)
                    qsrc, qdst, mi = qjobs[j]
                    src_ap = qt_sb[:, half:half + 2048]
                    if len(qdst.shape) == 3:
                        src_ap = src_ap.rearrange("p (k c) -> p k c", k=KDM)
                    e.dma_start(out=qdst, in_=src_ap).then_inc(s2p[j % 2], 16)

                for j in range(NJq):
                    half = (j % 2) * 2048
                    if j >= 2:
                        e.wait_ge(v2, j - 1)
                    qsrc, qdst, mi = qjobs[j]
                    dst_ap = qw_sb[:, half:half + 2048]
                    if len(qsrc.shape) == 3:
                        dst_ap = dst_ap.rearrange("p (k c) -> p k c", k=KDM)
                    e.dma_start(out=dst_ap, in_=qsrc).then_inc(l2p[j % 2], 16)
                    if j >= 2:
                        q_store(j - 2)
                q_store(NJq - 2)
                q_store(NJq - 1)
                e.wait_ge(s2p[0], 16 * ((NJq + 1) // 2))
                e.wait_ge(s2p[1], 16 * (NJq // 2))

            phase(tag, {"sync": b_sync, "scalar": b_scalar, "tensor": b_tensor,
                        "vector": b_vec, "gpsimd": b_gp})

    emit_ln("p5b", rs_dr, xnf_dr, xnb_dr, qjobs=qjobs_all)

    for hf in range(2):
        HW = 1024
        with contextlib.ExitStack() as _st:
            _en = _st.enter_context
            xnh_sb = _en(nc.sbuf_tensor(f"f{hf}xn", [128, KDM * HW], BF16))
            g_sb = _en(nc.sbuf_tensor(f"f{hf}g", [128, NHG * HW], BF16))
            t1s_sb = _en(nc.sbuf_tensor(f"f{hf}t1", [128, 2 * 2048], BF16))
            t3s_sb = _en(nc.sbuf_tensor(f"f{hf}t3", [128, 2 * 2048], BF16))
            sil_sb = _en(nc.sbuf_tensor(f"f{hf}sil", [128, 4 * 512], BF16))
            t2s_sb = _en(nc.sbuf_tensor(f"f{hf}t2", [128, 2 * 512], BF16))
            xf_sb = _en(nc.sbuf_tensor(f"f{hf}xf", [128, 8 * 512], F32))
            r2_sb = _en(nc.sbuf_tensor(f"f{hf}r2", [128, 8 * 512], F32))
            fp0 = _en(nc.psum_tensor(f"f{hf}p0", [128, 512], F32))
            fp1 = _en(nc.psum_tensor(f"f{hf}p1", [128, 512], F32))
            fp2 = _en(nc.psum_tensor(f"f{hf}p2", [128, 512], F32))
            fp3 = _en(nc.psum_tensor(f"f{hf}p3", [128, 512], F32))
            fp4 = _en(nc.psum_tensor(f"f{hf}p4", [128, 512], F32))
            fp5 = _en(nc.psum_tensor(f"f{hf}p5", [128, 512], F32))
            fp6 = _en(nc.psum_tensor(f"f{hf}p6", [128, 512], F32))
            fp7 = _en(nc.psum_tensor(f"f{hf}p7", [128, 512], F32))
            dbg_sb = _en(nc.sbuf_tensor(f"f{hf}dbg", [128, 512], F32))
            d6 = sem(f"f{hf}d6")
            lx6 = sem(f"f{hf}lx")
            lt6 = sem(f"f{hf}lt")
            m6 = sem(f"f{hf}m")
            s6 = sem(f"f{hf}s")
            v6 = sem(f"f{hf}v")
            l26 = sem(f"f{hf}l2")
            lf6 = sem(f"f{hf}lf")
            m26 = sem(f"f{hf}m2")
            v26 = sem(f"f{hf}v2")
            st6 = sem(f"f{hf}st")
            hps = [fp0, fp1, fp2, fp3]
            hps2 = [fp4, fp5, fp6, fp7]
            wps = [fp0, fp1, fp2, fp3, fp4, fp5, fp6, fp7]

            def p6_sync(e):
                for k in range(KDM):
                    e.dma_start(
                        out=xnh_sb[:, k * HW:(k + 1) * HW],
                        in_=xnb_dr[k * 128:(k + 1) * 128,
                                   hf * HW:(hf + 1) * HW]).then_inc(lx6, 16)
                for hg in range(NHG):
                    pr = hg % 2
                    if hg >= 2:
                        e.wait_ge(m6, 4 * (hg - 1))
                    e.dma_start(out=t1s_sb[:, pr * 2048:(pr + 1) * 2048]
                                .rearrange("p (k c) -> p k c", k=KDM),
                                in_=t1_dr[hg].rearrange("k p c -> p k c")
                                ).then_inc(lt6, 16)
                    e.dma_start(out=t3s_sb[:, pr * 2048:(pr + 1) * 2048]
                                .rearrange("p (k c) -> p k c", k=KDM),
                                in_=t3_dr[hg].rearrange("k p c -> p k c")
                                ).then_inc(lt6, 16)
                for dp in range(4):
                    for hg in range(NHG):
                        gi = dp * NHG + hg
                        pr = gi % 2
                        if gi >= 2:
                            e.wait_ge(m26, gi - 1)
                        e.dma_start(
                            out=t2s_sb[:, pr * 512:(pr + 1) * 512],
                            in_=t2_dr[hg][:, dp * 512:(dp + 1) * 512]).then_inc(l26, 16)
                    for dgo in range(4):
                        for sub in range(2):
                            dgg = dp * 4 + dgo
                            si = dgo * 2 + sub
                            if dp >= 1:
                                e.wait_ge(v26, 8 * (dp - 1) + si + 1)
                            e.dma_start(
                                out=xf_sb[:, si * 512:(si + 1) * 512],
                                in_=xnf_dr[dgg * 128:(dgg + 1) * 128,
                                           hf * HW + sub * 512:hf * HW + (sub + 1) * 512],
                            ).then_inc(lf6, 16)

            def p6_tensor(e):
                e.wait_ge(lx6, 16 * KDM)
                for hg in range(NHG):
                    pr = hg % 2
                    e.wait_ge(lt6, 32 * (hg + 1))
                    pset = hps if pr == 0 else hps2
                    if hg >= 2:
                        e.wait_ge(v6, 2 * (hg - 1))
                    for sub in range(2):
                        for k in range(KDM):
                            e.matmul(
                                pset[sub][:, :],
                                t1s_sb[:, pr * 2048 + k * 128:pr * 2048 + (k + 1) * 128],
                                xnh_sb[:, k * HW + sub * 512:k * HW + (sub + 1) * 512],
                                start=(k == 0), stop=(k == KDM - 1)
                                ).then_inc(m6, 1 if k == KDM - 1 else 0)
                        for k in range(KDM):
                            e.matmul(
                                pset[2 + sub][:, :],
                                t3s_sb[:, pr * 2048 + k * 128:pr * 2048 + (k + 1) * 128],
                                xnh_sb[:, k * HW + sub * 512:k * HW + (sub + 1) * 512],
                                start=(k == 0), stop=(k == KDM - 1)
                                ).then_inc(m6, 1 if k == KDM - 1 else 0)
                e.wait_ge(v6, 2 * NHG)
                for dp in range(4):
                    if dp >= 1:
                        e.wait_ge(v26, 8 * dp)
                    for hg in range(NHG):
                        gi = dp * NHG + hg
                        pr = gi % 2
                        e.wait_ge(l26, 16 * (gi + 1))
                        for dgo in range(4):
                            for sub in range(2):
                                e.matmul(
                                    wps[dgo * 2 + sub][:, :],
                                    t2s_sb[:, pr * 512 + dgo * 128:
                                           pr * 512 + (dgo + 1) * 128],
                                    g_sb[:, hg * HW + sub * 512:hg * HW + (sub + 1) * 512],
                                    start=(hg == 0), stop=(hg == NHG - 1),
                                ).then_inc(m26, 1 if (dgo == 3 and sub == 1) else 0)

            def p6_scalar(e):
                for hg in range(NHG):
                    pr = hg % 2
                    pset = hps if pr == 0 else hps2
                    for sub in range(2):
                        e.wait_ge(m6, 4 * hg + 2 * (sub + 1))
                        if hg >= 2:
                            e.wait_ge(v6, 2 * (hg - 1))
                        e.activation(sil_sb[:, (pr * 2 + sub) * 512:
                                            (pr * 2 + sub + 1) * 512],
                                     pset[sub][:, :], AF.Silu,
                                     scale=s1_sb[:, :]).then_inc(s6, 1)

            def p6_vec(e):
                for hg in range(NHG):
                    pr = hg % 2
                    pset = hps if pr == 0 else hps2
                    for sub in range(2):
                        e.wait_ge(s6, 2 * hg + sub + 1)
                        e.wait_ge(m6, 4 * hg + 4)
                        e.tensor_tensor(
                            g_sb[:, hg * HW + sub * 512:hg * HW + (sub + 1) * 512],
                            sil_sb[:, (pr * 2 + sub) * 512:(pr * 2 + sub + 1) * 512],
                            pset[2 + sub][:, :], ALU.mult).then_inc(v6, 1)
                dbg_w2 = os.environ.get("KDEBUG") == "1" and hf == 0
                for dp in range(4):
                    if dp == 0 and dbg_w2:
                        e.wait_ge(m26, NHG)
                        e.tensor_copy(dbg_sb[:, :], wps[0][:, :]).then_inc(d6, 1)
                    if dp == 1 and dbg_w2:
                        e.wait_ge(m26, 2 * NHG)
                        e.wait_ge(d6, 33)
                        e.tensor_copy(dbg_sb[:, :], wps[2][:, :]).then_inc(d6, 1)
                    for dgo in range(4):
                        for sub in range(2):
                            si = dgo * 2 + sub
                            e.wait_ge(m26, (dp + 1) * NHG)
                            if si == 0:
                                e.wait_ge(lf6, 16 * 8 * (dp + 1))
                                if dp >= 1:
                                    e.wait_ge(st6, 16 * 8 * dp)
                            e.scalar_tensor_tensor(
                                r2_sb[:, si * 512:(si + 1) * 512],
                                wps[si][:, :], s23_sb[:, :],
                                xf_sb[:, si * 512:(si + 1) * 512],
                                ALU.mult, ALU.add).then_inc(v26, 1)

            def p6_sync2(e):
                for dp in range(4):
                    for dgo in range(4):
                        for sub in range(2):
                            si = dgo * 2 + sub
                            dgg = dp * 4 + dgo
                            e.wait_ge(v26, 8 * dp + si + 1)
                            e.dma_start(
                                out=rs2_dr[dgg * 128:(dgg + 1) * 128,
                                           hf * HW + sub * 512:hf * HW + (sub + 1) * 512],
                                in_=r2_sb[:, si * 512:(si + 1) * 512]).then_inc(st6, 16)
                e.wait_ge(st6, 16 * 32)

            def p6_scalar_all(e):
                p6_scalar(e)
                if os.environ.get("KDEBUG") == "1" and hf == 0:
                    e.wait_ge(v6, 2)
                    e.dma_start(out=dbg_g_dr[:, :], in_=g_sb[:, 0:512]).then_inc(d6, 16)
                    e.wait_ge(d6, 17)
                    e.dma_start(out=dbg_ps_dr[:, :], in_=dbg_sb[:, :]).then_inc(d6, 16)
                    e.nop().then_inc(d6, 1)
                    e.wait_ge(d6, 35)
                    e.dma_start(out=dbg_ps2_dr[:, :], in_=dbg_sb[:, :]).then_inc(d6, 16)
                    e.wait_ge(d6, 51)
                p6_sync2(e)

            phase(f"p6_{hf}", {"sync": p6_sync, "tensor": p6_tensor,
                               "scalar": p6_scalar_all, "vector": p6_vec})

    emit_ln("p7", rs2_dr, out_d, None, dbg=True)

    if os.environ.get("KDEBUG") == "1":
        dbg_specs = [
            ("d_xt", xt_dr[0, :, 0:512], [128, 512], BF16),
            ("d_sg", sg_dr[0, :, 0:512], [128, 512], BF16),
            ("d_ssm", ssm_fl[0][:, 0:512], [160, 512], F32),
            ("d_y2", y2_in[0][0:128, 0:512], [128, 512], BF16),
            ("d_y2my", y2my[0:128, 0:512], [128, 512], BF16),
            ("d_rs", rs_dr[0:128, 0:512], [128, 512], F32),
            ("d_xnf", xnf_dr[0:128, 0:512], [128, 512], F32),
            ("d_t1", t1_dr[0].rearrange("k p c -> p k c"), [128, 16, 128], BF16),
            ("d_thp", None, [128, 3], F32),
            ("d_rs2", rs2_dr[0:128, 0:512], [128, 512], F32),
            ("d_absi", abs_in[:, :], [1, 8], F32),
            ("d_abso", abs_out[:, :], [1, 8], F32),
            ("d_xnb", xnb_dr[0:128, 0:512], [128, 512], BF16),
            ("d_t2", t2_dr[0][:, 0:512], [128, 512], BF16),
            ("d_g", dbg_g_dr[:, :], [128, 512], BF16),
            ("d_ps", dbg_ps_dr[:, :], [128, 512], F32),
            ("d_s1", None, [128, 1], F32),
            ("d_s23", None, [128, 1], F32),
            ("d_ps2", dbg_ps2_dr[:, :], [128, 512], F32),
            ("d_rs2b", rs2_dr[640:768, 0:512], [128, 512], F32),
        ] + [(f"d_l{nm}", t[:, :], [128, 512], F32) for nm, t in _LND.items()]
        douts = {}
        for nm, ap, shape, dt in dbg_specs:
            douts[nm] = nc.dram_tensor(nm, shape, dt, kind="ExternalOutput")
        dbg_sem = sem("dbg")

        def pdbg_sync(e):
            n = 0
            for nm, ap, shape, dt in dbg_specs:
                if ap is None:
                    continue
                e.dma_start(out=douts[nm].ap(), in_=ap).then_inc(dbg_sem, 16)
                n += 1
            e.dma_start(out=douts["d_thp"][:, :], in_=thp_sb[:, :]).then_inc(dbg_sem, 16)
            e.dma_start(out=douts["d_s1"][:, :], in_=s1_sb[:, :]).then_inc(dbg_sem, 16)
            e.dma_start(out=douts["d_s23"][:, :], in_=s23_sb[:, :]).then_inc(dbg_sem, 16)
            n += 3
            e.wait_ge(dbg_sem, 16 * n)

        phase("pdbg", {"sync": pdbg_sync})

    return nc, stack


_CACHE = {}


def _prep_inputs(inputs):
    f32 = np.float32
    x = np.ascontiguousarray(np.asarray(inputs["L0_attn_hidden_states"], f32)[0].T)
    Winp = np.asarray(inputs["L0_attn_in_proj_w"], f32)
    convw = np.asarray(inputs["L0_attn_conv_w"], f32)
    convb = np.asarray(inputs["L0_attn_conv_b"], f32)
    Wxp = np.asarray(inputs["L0_attn_x_proj_w"], f32)
    Wdt = np.asarray(inputs["L0_attn_dt_proj_w"], f32)
    bdt = np.asarray(inputs["L0_attn_dt_proj_b"], f32)
    A = np.asarray(inputs["L0_attn_A"], f32)
    Dp = np.asarray(inputs["L0_attn_D"], f32)
    Wout = np.asarray(inputs["L0_attn_out_proj_w"], f32)
    w1 = np.asarray(inputs["L0_ffn_w1"], f32)
    w3 = np.asarray(inputs["L0_ffn_w3"], f32)
    w2 = np.asarray(inputs["L0_ffn_w2"], f32)

    def blk_T(w):
        wt = np.zeros((DM, HP), f32)
        wt[:, :H] = w.T
        return np.ascontiguousarray(
            wt.reshape(KDM, 128, NHG, 128).transpose(2, 0, 1, 3))

    w1b = blk_T(w1)
    w3b = blk_T(w3)
    w2t = np.zeros((HP, DM), f32)
    w2t[:H] = w2.T
    w2b = np.ascontiguousarray(w2t.reshape(NHG, 128, DM))

    wot = np.ascontiguousarray(Wout.T)
    w_outb = np.ascontiguousarray(
        wot.reshape(32, 128, 16, 128).transpose(2, 0, 1, 3))

    def parts(w):
        fl = np.ascontiguousarray(w).reshape(-1)
        per = fl.size // NCORES
        return [np.ascontiguousarray(fl[i * per:(i + 1) * per].reshape(128, per // 128))
                for i in range(NCORES)]

    p1s, p3s, p2s = parts(w1), parts(w3), parts(w2)

    in_maps = []
    for c in range(NCORES):
        sl = slice(c * DS, (c + 1) * DS)
        wi = np.ascontiguousarray(
            np.concatenate([Winp[sl], Winp[DI + c * DS:DI + (c + 1) * DS]], 0).T)
        d = {
            "x_all": x,
            "x_res": np.ascontiguousarray(x[:, c * TC:(c + 1) * TC]),
            "w_in": wi,
            "convw": np.ascontiguousarray(convw[sl].reshape(NDG, 128, 4)
                                          .transpose(1, 0, 2).reshape(128, 4 * NDG)),
            "convb": np.ascontiguousarray(convb[sl].reshape(NDG, 128).T),
            "a_mat": np.ascontiguousarray(A[sl].reshape(NDG, 128, S)
                                          .transpose(1, 0, 2).reshape(128, NDG * S)),
            "d_vec": np.ascontiguousarray(Dp[sl].reshape(NDG, 128).T),
            "dt_bv": np.ascontiguousarray(bdt[sl].reshape(NDG, 128).T),
            "w_xp": np.ascontiguousarray(Wxp[:, sl].T),
            "w_dt": np.ascontiguousarray(Wdt[sl].T),
            "w_outb": w_outb,
            "w1b": w1b, "w3b": w3b, "w2b": w2b,
            "w1p": p1s[c], "w3p": p3s[c], "w2p": p2s[c],
            "tok0": np.array([[c * TC]], dtype=np.uint32),
        }
        in_maps.append(d)
    return in_maps


def kernel(**inputs):
    in_maps = _prep_inputs(inputs)
    if "nc" not in _CACHE:
        _CACHE["nc"] = build_nc()
    nc, _stack = _CACHE["nc"]
    res = run_bass_kernel_spmd(nc, in_maps, core_ids=list(range(NCORES)))
    parts = [res.results[c]["out_d"] for c in range(NCORES)]
    full = np.concatenate(parts, axis=1)
    return np.ascontiguousarray(full.T)[None].astype(np.float32)
